# version 15
# baseline (speedup 1.0000x reference)
import numpy as np
import concourse.bass as bass
import concourse.mybir as mybir
from concourse.bass_utils import run_bass_kernel_spmd

F32 = mybir.dt.float32
BF16 = mybir.dt.bfloat16
AF = mybir.ActivationFunctionType
ALU = mybir.AluOpType

D = 1024
H = 8
C = 128
KC = 8
NEG = -1.0e30
BIG = float(2.0 ** 30)
NINV = 7
TF = 2048

ENGS = ("sync", "gpsimd", "tensor", "vector", "scalar")


class Em:

    def __init__(self):
        self.ops = []
        self.lw = {}
        self.rd = {}

    def op(self, eng, fn, reads=(), writes=(), dma=False, tag=""):
        deps = set()
        for r in reads:
            if r in self.lw:
                deps.add(self.lw[r])
        for w in writes:
            if w in self.lw:
                deps.add(self.lw[w])
            deps.update(self.rd.get(w, ()))
        i = len(self.ops)
        assert not dma or len(writes) >= 1
        self.ops.append({"i": i, "eng": eng, "fn": fn, "deps": deps,
                         "dma": dma, "tag": tag,
                         "key": ("d_" + writes[0]) if dma else eng})
        for r in reads:
            self.rd.setdefault(r, set()).add(i)
        for w in writes:
            self.lw[w] = i
            self.rd[w] = set()
        return i

    def finalize(self):
        n = len(self.ops)
        self.needs_inc = [False] * n
        for o in self.ops:
            if o["dma"]:
                self.needs_inc[o["i"]] = True
        for o in self.ops:
            for d in o["deps"]:
                if self.ops[d]["eng"] != o["eng"] or self.ops[d]["dma"]:
                    self.needs_inc[d] = True
        self.val = {}
        cnt = {}
        for o in self.ops:
            if self.needs_inc[o["i"]]:
                k = o["key"]
                cnt[k] = cnt.get(k, 0) + (16 if o["dma"] else 1)
                self.val[o["i"]] = cnt[k]
        self.final = cnt
        self.dma_keys = {}
        for o in self.ops:
            if o["dma"]:
                self.dma_keys[o["key"]] = o["eng"]
        return sorted(self.dma_keys)

    def run_engine(self, nc, engobj, ename, sems):
        waited = {k: 0 for k in sems}
        drained_upto = -1
        last_own = -1
        for o in self.ops:
            if o["eng"] != ename:
                continue
            need_drain = False
            wneed = {}
            for d in o["deps"]:
                dd = self.ops[d]
                if dd["eng"] == ename and not dd["dma"]:
                    if ename != "tensor" and d > drained_upto:
                        need_drain = True
                else:
                    k = dd["key"]
                    v = self.val[d]
                    if v > waited[k]:
                        wneed[k] = max(wneed.get(k, 0), v)
            if need_drain:
                engobj.drain()
                drained_upto = last_own
            for k, v in sorted(wneed.items()):
                engobj.wait_ge(sems[k], v)
                waited[k] = v
            inst = o["fn"](nc, engobj)
            if self.needs_inc[o["i"]]:
                inst.then_inc(sems[o["key"]], 16 if o["dma"] else 1)
            last_own = o["i"]
        for k, e in sorted(self.dma_keys.items()):
            if e == ename and self.final.get(k, 0) > waited[k]:
                engobj.wait_ge(sems[k], self.final[k])


def host_constants():
    c = {}
    c["c_ident"] = np.eye(128, dtype=np.float32)
    c["c_meanones"] = np.full((128, 1), 1.0 / D, dtype=np.float32)
    hs = np.zeros((128, KC * H), dtype=np.float32)
    for kc in range(KC):
        hs[:, kc * H + kc] = 1.0
    c["c_headsel"] = hs
    sel = np.zeros((H, H * 128), dtype=np.float32)
    for h in range(H):
        sel[h, h * 128:(h + 1) * 128] = 1.0
    c["c_selrow"] = sel
    ii = np.arange(C)
    m1 = np.where(ii[:, None] > ii[None, :], 0.0, BIG).astype(np.float32)
    m2 = np.where(ii[None, :] > ii[:, None], 0.0, -BIG).astype(np.float32)
    m3 = np.where(ii[None, :] >= ii[:, None], 0.0, -BIG).astype(np.float32)
    c["c_maskb"] = np.concatenate([m1, m2, m3], axis=1)
    sel4 = np.zeros((4, 4 * 128), dtype=np.float32)
    for e in range(4):
        sel4[e, e * 128:(e + 1) * 128] = 1.0
    c["c_sel4"] = sel4
    return c


def host_shared(inputs):
    d = {}
    d["wwT"] = np.ascontiguousarray(np.asarray(inputs["L0_attn_W_write"]).T)
    d["woT"] = np.ascontiguousarray(np.asarray(inputs["L0_attn_W_out"]).T)
    d["wgbaT"] = np.ascontiguousarray(np.concatenate(
        [np.asarray(inputs["L0_attn_W_gate"]),
         np.asarray(inputs["L0_attn_W_beta"]),
         np.asarray(inputs["L0_attn_W_alpha"])], axis=0).T)
    d["dtb"] = np.ascontiguousarray(np.asarray(inputs["L0_attn_dt_bias"]).reshape(H, 1))
    d["alg"] = np.ascontiguousarray(np.asarray(inputs["L0_attn_A_log"]).reshape(H, 1))
    d["wr"] = np.ascontiguousarray(np.asarray(inputs["L0_ffn_Wr"]))
    d["brr"] = np.ascontiguousarray(np.broadcast_to(
        np.asarray(inputs["L0_ffn_br"]).reshape(1, 4), (128, 4)))
    d["we1"] = np.ascontiguousarray(np.asarray(inputs["L0_ffn_We1"]))
    be1 = np.asarray(inputs["L0_ffn_be1"])
    d["be1c"] = np.ascontiguousarray(
        be1.reshape(4, 32, 128).transpose(2, 0, 1).reshape(128, 128))
    d["we2"] = np.ascontiguousarray(np.asarray(inputs["L0_ffn_We2"]))
    be2 = np.asarray(inputs["L0_ffn_be2"])
    be2e = np.zeros((4, 4, 1024), dtype=np.float32)
    for e in range(4):
        be2e[e, e, :] = be2[e]
    d["be2e"] = be2e.reshape(4, 4096)
    return d


def build_nc(T, dbg=False):
    NCH = T // C
    NTS = max(T // 512, 1)
    TS = min(512, T)
    NTT = T // 128
    nc = bass.Bass("TRN2", target_bir_lowering=False)

    dr = {}
    def din(name, shape):
        dr[name] = nc.dram_tensor(name, list(shape), F32, kind="ExternalInput").ap()
    din("xT", (D, T))
    din("wwT", (D, D)); din("woT", (D, D)); din("wgbaT", (D, 24))
    din("dtb", (H, 1)); din("alg", (H, 1))
    din("wr", (D, 4)); din("brr", (128, 4))
    din("we1", (4, D, 4096)); din("be1c", (128, 128))
    din("we2", (4, 4096, D)); din("be2e", (4, 4096))
    for k, v in host_constants().items():
        din(k, v.shape)
    out_d = nc.dram_tensor("out", [D, T], F32, kind="ExternalOutput").ap()

    em = Em()

    from contextlib import ExitStack
    with ExitStack() as _st:
        big0 = _st.enter_context(nc.sbuf_tensor("big0", [128, 8 * T], F32))
        big1 = _st.enter_context(nc.sbuf_tensor("big1", [128, 16384], BF16))
        big2 = _st.enter_context(nc.sbuf_tensor("big2", [128, KC, T + 1], BF16))
        big3 = _st.enter_context(nc.sbuf_tensor("big3", [128, 32768], BF16))
        wstr = _st.enter_context(nc.sbuf_tensor("wstr", [128, 1, KC, 128], BF16))
        tscl32 = _st.enter_context(nc.sbuf_tensor("tscl32", [128, NCH * 40], F32))
        tscl16 = _st.enter_context(nc.sbuf_tensor("tscl16", [128, NCH * 24], BF16))
        ssc = _st.enter_context(nc.sbuf_tensor("ssc", [128, H * NCH], F32))
        bndt = _st.enter_context(nc.sbuf_tensor("bndt", [H, NCH], F32))
        sscale = _st.enter_context(nc.sbuf_tensor("sscale", [H, NCH], F32))
        expa = _st.enter_context(nc.sbuf_tensor("expa", [H, 1], F32))
        ndtb = _st.enter_context(nc.sbuf_tensor("ndtb", [H, 1], F32))
        dtb_sb = _st.enter_context(nc.sbuf_tensor("dtb_sb", [H, 1], F32))
        alg_sb = _st.enter_context(nc.sbuf_tensor("alg_sb", [H, 1], F32))
        Ssta = _st.enter_context(nc.sbuf_tensor("Ssta", [128, 128], F32))
        Sb = _st.enter_context(nc.sbuf_tensor("Sbf", [128, 128], BF16))
        u1t = _st.enter_context(nc.sbuf_tensor("u1t", [128, 128], BF16))
        o1t = _st.enter_context(nc.sbuf_tensor("o1t", [128, 128], BF16))
        otm = _st.enter_context(nc.sbuf_tensor("otm", [128, 128], BF16))
        attm = _st.enter_context(nc.sbuf_tensor("attm", [128, 128], BF16))
        kwdw = _st.enter_context(nc.sbuf_tensor("kwdw", [128, 128], BF16))
        vvt = _st.enter_context(nc.sbuf_tensor("vvt", [128, 2, 128], BF16))
        qbt = _st.enter_context(nc.sbuf_tensor("qbt", [128, 4, 128], BF16))
        pbt = _st.enter_context(nc.sbuf_tensor("pbt", [128, 4, 128], BF16))
        qibt = _st.enter_context(nc.sbuf_tensor("qibt", [128, 4, 128], BF16))
        wgba_sb = _st.enter_context(nc.sbuf_tensor("wgba_sb", [128, KC, 24], BF16))
        brr_sb = _st.enter_context(nc.sbuf_tensor("brr_sb", [128, 4], F32))
        mrep = _st.enter_context(nc.sbuf_tensor("mrep", [128, 512], BF16))
        wr32 = _st.enter_context(nc.sbuf_tensor("wr32", [128, KC, 4], F32))
        negcs = _st.enter_context(nc.sbuf_tensor("negcs", [1, 4], F32))
        rstdc = _st.enter_context(nc.sbuf_tensor("rstdc", [128, 1], F32))
        vones = _st.enter_context(nc.sbuf_tensor("vones", [8, 1], F32))
        c_ident = _st.enter_context(nc.sbuf_tensor("sb_c_ident", [128, 128], F32))
        c_identb = _st.enter_context(nc.sbuf_tensor("sb_c_identb", [128, 128], BF16))
        c_meanones = _st.enter_context(nc.sbuf_tensor("sb_c_meanones", [128, 1], F32))
        c_headsel = _st.enter_context(nc.sbuf_tensor("sb_c_headsel", [128, H], F32))
        c_selrow = _st.enter_context(nc.sbuf_tensor("sb_c_selrow", [H, 128], F32))
        be1c_sb = _st.enter_context(nc.sbuf_tensor("be1c_sb", [128, 32], F32))
        pq0 = _st.enter_context(nc.psum_tensor("pq0", [128, 512], F32))
        pq1 = _st.enter_context(nc.psum_tensor("pq1", [128, 512], F32))
        pq2 = _st.enter_context(nc.psum_tensor("pq2", [128, 512], F32))
        pq3 = _st.enter_context(nc.psum_tensor("pq3", [128, 512], F32))
        pq4 = _st.enter_context(nc.psum_tensor("pq4", [128, 512], F32))
        pq5 = _st.enter_context(nc.psum_tensor("pq5", [128, 512], F32))
        pq6 = _st.enter_context(nc.psum_tensor("pq6", [128, 512], F32))
        pq7 = _st.enter_context(nc.psum_tensor("pq7", [128, 512], F32))
        s_sync = _st.enter_context(nc.semaphore("s_sync"))
        s_gp = _st.enter_context(nc.semaphore("s_gpsimd"))
        s_t = _st.enter_context(nc.semaphore("s_tensor"))
        s_v = _st.enter_context(nc.semaphore("s_vector"))
        s_s = _st.enter_context(nc.semaphore("s_scalar"))
        sems = {"sync": s_sync, "gpsimd": s_gp, "tensor": s_t,
                "vector": s_v, "scalar": s_s}
        pk = [pq0, pq1, pq2, pq3, pq4, pq5, pq6, pq7]
        PKR = ["pq%d" % i for i in range(8)]
        sb_const = {"c_ident": c_ident, "c_meanones": c_meanones,
                    "c_selrow": c_selrow}

        def sdma(dst, src, reads, writes, tag=""):
            em.op("sync", lambda nc, eng, d=dst, s=src: eng.dma_start(d, s),
                  reads=reads, writes=writes, dma=True, tag=tag)

        def gdma(dst, src, reads, writes, tag=""):
            em.op("gpsimd", lambda nc, eng, d=dst, s=src: eng.dma_start(d, s),
                  reads=reads, writes=writes, dma=True, tag=tag)

        DBG = {}
        def ddump(name, src_ap, reads):
            if not dbg:
                return
            shp = list(src_ap.shape)
            dt_ = src_ap.dtype
            t_ = nc.dram_tensor("dbg_" + name, shp, dt_, kind="ExternalOutput").ap()
            DBG[name] = t_
            em.op("sync", lambda nc, eng, d=t_, s=src_ap: eng.dma_start(d, s),
                  reads=reads, writes=("dbg_" + name,), dma=True, tag="dbg_" + name)

        def mm(dst, lhsT, rhs, start, stop, reads, writes, tag=""):
            em.op("tensor",
                  lambda nc, eng, d=dst, l=lhsT, r=rhs, s1=start, s2=stop:
                  nc.tensor.matmul(d, l, r, start=s1, stop=s2),
                  reads=reads, writes=writes, tag=tag)

        def trans(dst, src, ident, reads, writes, tag=""):
            em.op("tensor",
                  lambda nc, eng, d=dst, s=src, i=ident:
                  nc.tensor.transpose(d, s, i),
                  reads=reads, writes=writes, tag=tag)

        def vop(fn, reads, writes, tag=""):
            em.op("vector", fn, reads=reads, writes=writes, tag=tag)

        def sop(fn, reads, writes, tag=""):
            em.op("scalar", fn, reads=reads, writes=writes, tag=tag)

        b0 = big0.ap()
        def xT_sl(kc, lo, hi):
            return b0[:, kc * T + lo: kc * T + hi]
        x2_sl = xT_sl
        RS = {"gate": 0, "beta": 1, "lnbeta": 2, "dec": 3, "sufexp": 4,
              "dexp": 5, "nrm": 6, "rwsh": 7}
        def row(name, lo=0, hi=None):
            hi = T if hi is None else hi
            return b0[0:H, RS[name] * T + lo: RS[name] * T + hi]

        b1 = big1.ap()
        def vT_sl(kc, lo, hi):
            return b1[:, kc * T + lo: kc * T + hi]
        def h1m_sl(fc, lo, hi):
            return b1[:, fc * TS + lo: fc * TS + hi]

        b2 = big2.ap()
        def kw_sl(h, lo, hi):
            return b2[:, h, lo:hi]
        def k_sl(h, lo, hi):
            return b2[:, h, 1 + lo:1 + hi]

        b3 = big3.ap()
        b3f = big3.ap().bitcast(F32)
        def oT_sl(kc, lo, hi):
            return b3[:, kc * T + lo: kc * T + hi]
        tmp_sq = b3f[:, 0:T]
        tmpA = b3f[:, TF:TF + 512]
        tmpF = b3f[:, 4 * TF:4 * TF + 512]
        def t1(lo=0, hi=None):
            hi = T if hi is None else hi
            return b3f[0:H, TF + lo: TF + hi]
        def t2(lo=0, hi=None):
            hi = T if hi is None else hi
            return b3f[0:H, 2 * TF + lo: 2 * TF + hi]
        def t3(lo=0, hi=None):
            hi = T if hi is None else hi
            return b3f[0:H, 3 * TF + lo: 3 * TF + hi]
        def srow(i, lo=0, hi=None, p=1):
            hi = T if hi is None else hi
            return b3f[0:p, (4 + i) * TF + lo: (4 + i) * TF + hi]
        L_OFF = 7 * TF
        def Lsl(fam, lo, hi):
            return b3f[:, L_OFF + (fam - 1) * 512 + lo: L_OFF + (fam - 1) * 512 + hi]
        def we1b(buf):
            return b3[:, buf * 4096:(buf + 1) * 4096].rearrange(
                "p (k f) -> p k f", k=KC)
        def we2b(buf):
            return b3[:, 8192 + buf * 4096: 8192 + (buf + 1) * 4096].rearrange(
                "p (f j) -> p f j", f=4)
        maskfm = b3[0:4, 28672:28672 + T]
        be2e_sb = b3[0:4, 31232:32256]
        sel4_sb = b3[0:4, 32256:32384]
        c_maskb = b3[:, 32384:32768]

        wst = wstr.ap()

        sdma(b0.rearrange("p (c t) -> p c t", c=8), dr["xT"].rearrange("(c p) t -> p c t", p=128),
             reads=(), writes=("xT",), tag="ld_x")
        for nm, sb in sb_const.items():
            if nm == "c_selrow":
                sdma(sb[:], dr[nm][:, 0:128], reads=(), writes=(nm,), tag="ld_selrow0")
                continue
            if nm == "c_maskb":
                gdma(sb[:], dr[nm][:], reads=(), writes=(nm,), tag="ld_" + nm)
            else:
                sdma(sb[:], dr[nm][:], reads=(), writes=(nm,), tag="ld_" + nm)
        sdma(brr_sb[:], dr["brr"][:], reads=(), writes=("brr",))
        sdma(dtb_sb[:], dr["dtb"][:], reads=(), writes=("dtb",))
        sdma(alg_sb[:], dr["alg"][:], reads=(), writes=("alg",))
        gdma(c_identb[:], dr["c_ident"][:], reads=(), writes=("c_identb",))
        gdma(c_maskb, dr["c_maskb"][:], reads=(), writes=("c_maskb",))
        gdma(wgba_sb[:], dr["wgbaT"].rearrange("(k p) c -> p k c", p=128),
             reads=(), writes=("wgba",))
        sdma(wr32[:], dr["wr"].rearrange("(k p) c -> p k c", p=128),
             reads=(), writes=("wr32",))
        vop(lambda nc, eng: nc.vector.memset(vones[:], 1.0),
            reads=(), writes=("vones",), tag="vones")
        sop(lambda nc, eng: nc.scalar.activation(expa[:], alg_sb[:], AF.Exp),
            reads=("alg",), writes=("expa",), tag="expa")
        sop(lambda nc, eng: nc.scalar.mul(ndtb[:], dtb_sb[:], -1.0),
            reads=("dtb",), writes=("ndtb",), tag="ndtb")

        def emit_ln(src_region, src_sl, dst_region, subkey=False, pad=True):
            for kc in range(KC):
                for t in range(NTS):
                    mm(pk[t][0:1, 0:TS], c_meanones[:, 0:1],
                       src_sl(kc, t * TS, (t + 1) * TS),
                       start=(kc == 0), stop=(kc == KC - 1),
                       reads=("c_meanones", src_region), writes=(PKR[t],),
                       tag="ln_mean")
            for kc in range(KC):
                sop(lambda nc, eng, s=src_sl(kc, 0, T):
                    nc.scalar.activation(tmp_sq[:, 0:T], s, AF.Square),
                    reads=(src_region,), writes=("sqtmp",), tag="ln_sq")
                for t in range(NTS):
                    mm(pk[4 + t][0:1, 0:TS], c_meanones[:, 0:1],
                       tmp_sq[:, t * TS:(t + 1) * TS],
                       start=(kc == 0), stop=(kc == KC - 1),
                       reads=("c_meanones", "sqtmp"), writes=(PKR[4 + t],),
                       tag="ln_ssq")
            for t in range(NTS):
                sop(lambda nc, eng, t_=t: nc.scalar.copy(
                    srow(0, t_ * TS, (t_ + 1) * TS), pk[t_][0:1, 0:TS]),
                    reads=(PKR[t],), writes=("stats",), tag="ln_mcp")
                sop(lambda nc, eng, t_=t: nc.scalar.copy(
                    srow(1, t_ * TS, (t_ + 1) * TS), pk[4 + t_][0:1, 0:TS]),
                    reads=(PKR[4 + t],), writes=("stats",), tag="ln_scp")
            vop(lambda nc, eng: nc.vector.tensor_mul(
                srow(2), srow(0), srow(0)),
                reads=("stats",), writes=("stats",), tag="ln_m2")
            vop(lambda nc, eng: nc.vector.tensor_sub(srow(1), srow(1), srow(2)),
                reads=("stats",), writes=("stats",), tag="ln_var")
            vop(lambda nc, eng: nc.vector.tensor_scalar_add(
                srow(1), srow(1), 1e-5),
                reads=("stats",), writes=("stats",), tag="ln_eps")
            sop(lambda nc, eng: nc.scalar.activation(
                srow(1), srow(1), AF.Sqrt),
                reads=("stats",), writes=("stats",), tag="ln_sqrt")
            vop(lambda nc, eng: nc.vector.reciprocal(srow(1), srow(1)),
                reads=("stats",), writes=("stats",), tag="ln_rcp")
            vop(lambda nc, eng: nc.vector.tensor_mul(srow(2), srow(0), srow(1)),
                reads=("stats",), writes=("stats",), tag="ln_mr")
            sdma(c_selrow[:], dr["c_selrow"][:, 0:128],
                 reads=(), writes=("c_selrow",), tag="ld_selrow_ln")
            for t in range(NTS):
                mm(pk[6][:, 0:TS], c_selrow[0:1, 0:128], srow(1, t * TS, (t + 1) * TS),
                   start=True, stop=True, reads=("c_selrow", "stats"),
                   writes=(PKR[6],), tag="ln_rrep")
                mm(pk[7][:, 0:TS], c_selrow[0:1, 0:128], srow(2, t * TS, (t + 1) * TS),
                   start=True, stop=True, reads=("c_selrow", "stats"),
                   writes=(PKR[7],), tag="ln_mrep")
                for kc in range(KC):
                    vop(lambda nc, eng, s=src_sl(kc, t * TS, (t + 1) * TS):
                        nc.vector.tensor_mul(tmpA[:, 0:TS], s, pk[6][:, 0:TS]),
                        reads=(src_region, PKR[6]), writes=("rowsB",), tag="ln_a1")
                    vop(lambda nc, eng, kc_=kc, t_=t:
                        nc.vector.tensor_sub(
                            b2[:, kc_, 1 + t_ * TS: 1 + (t_ + 1) * TS],
                            tmpA[:, 0:TS], pk[7][:, 0:TS]),
                        reads=("rowsB", PKR[7]),
                        writes=(dst_region,) + (("%st%d" % (dst_region, t),)
                                                if subkey else ()),
                        tag="ln_a2")
            if pad:
                for kc in range(KC):
                    vop(lambda nc, eng, kc_=kc: nc.vector.memset(b2[:, kc_, 0:1], 0.0),
                        reads=(), writes=(dst_region,), tag="ln_pad")

        emit_ln("xT", xT_sl, "ln1")
        def ln1_rhs(kc, lo, hi):
            return b2[:, kc, 1 + lo: 1 + hi]

        ddump("ln1", b2[:, :, :], ("ln1",))
        for jc in range(KC):
            buf = 0
            gdma(wst[:, buf], dr["wwT"].rearrange("(k p) j -> p k j", p=128)
                 [:, :, jc * 128:(jc + 1) * 128],
                 reads=(), writes=("ws%d" % buf,), tag="ld_ww")
            for t in range(NTS):
                pb, pbr = pk[(jc * NTS + t) % 2], PKR[(jc * NTS + t) % 2]
                for kc in range(KC):
                    mm(pb[:, 0:TS], wst[:, buf, kc, :],
                       ln1_rhs(kc, t * TS, (t + 1) * TS),
                       start=(kc == 0), stop=(kc == KC - 1),
                       reads=("ws%d" % buf, "ln1"), writes=(pbr,), tag="vproj")
                sop(lambda nc, eng, jc_=jc, t_=t, pb_=pb:
                    nc.scalar.copy(vT_sl(jc_, t_ * TS, (t_ + 1) * TS), pb_[:, 0:TS]),
                    reads=(pbr,), writes=("vT",), tag="vproj_ev")

        for t in range(NTS):
            for g in range(3):
                pb, pbr = pk[2 + g], PKR[2 + g]
                for kc in range(KC):
                    mm(pb[0:H, 0:TS], wgba_sb[:, kc, g * 8:(g + 1) * 8],
                       ln1_rhs(kc, t * TS, (t + 1) * TS),
                       start=(kc == 0), stop=(kc == KC - 1),
                       reads=("wgba", "ln1"), writes=(pbr,), tag="gba")
            t0w = ("xT",) if t == 0 else ()
            sop(lambda nc, eng, t_=t: nc.scalar.activation(
                row("gate", t_ * TS, (t_ + 1) * TS), pk[2][0:H, 0:TS], AF.Sigmoid),
                reads=(PKR[2],), writes=("rows",) + t0w, tag="gate")
            sop(lambda nc, eng, t_=t: nc.scalar.activation(
                row("beta", t_ * TS, (t_ + 1) * TS), pk[3][0:H, 0:TS], AF.Sigmoid),
                reads=(PKR[3],), writes=("rows",), tag="beta")
            sop(lambda nc, eng, t_=t: nc.scalar.activation(
                row("dec", t_ * TS, (t_ + 1) * TS), pk[4][0:H, 0:TS],
                AF.Sigmoid, bias=ndtb[:, 0:1], scale=-1.0),
                reads=(PKR[4], "ndtb"), writes=("rows",), tag="siga")

        sop(lambda nc, eng: nc.scalar.activation(row("lnbeta"), row("beta"), AF.Ln),
            reads=("rows",), writes=("rows",), tag="lnbeta")
        sop(lambda nc, eng: nc.scalar.activation(row("dec"), row("dec"), AF.Ln),
            reads=("rows",), writes=("rows",), tag="lnsig")
        vop(lambda nc, eng: nc.vector.tensor_scalar_mul(
            row("dec"), row("dec"), expa[:, 0:1]),
            reads=("rows", "expa"), writes=("rows",), tag="decay")
        ddump("decay", row("dec"), ("rows",))
        for n in range(NCH):
            vop(lambda nc, eng, n_=n: nc.vector.tensor_tensor_scan(
                row("dec", n_ * C, (n_ + 1) * C),
                vones[:, 0:1].broadcast_to([H, C]),
                row("dec", n_ * C, (n_ + 1) * C), 0.0, ALU.mult, ALU.add),
                reads=("rows", "vones"), writes=("rows",), tag="scan")
        ddump("cum", row("dec"), ("rows",))
        vop(lambda nc, eng: nc.vector.tensor_copy(bndt[:], row("dec")[:, C - 1::C]),
            reads=("rows",), writes=("bnd",), tag="bnd")
        vop(lambda nc, eng: nc.vector.tensor_tensor(
            row("sufexp").rearrange("p (n c) -> p n c", c=C),
            bndt[:, :, None].broadcast_to([H, NCH, C]),
            row("dec").rearrange("p (n c) -> p n c", c=C), ALU.subtract),
            reads=("rows", "bnd"), writes=("rows",), tag="sufdec")
        ddump("sufd", row("sufexp"), ("rows",))
        ddump("bndt", bndt[:, :], ("bnd",))
        sop(lambda nc, eng: nc.scalar.activation(row("sufexp"), row("sufexp"), AF.Exp),
            reads=("rows",), writes=("rows",), tag="sufexp")
        ddump("decf", row("dec"), ("rows",))
        sop(lambda nc, eng: nc.scalar.activation(row("dexp"), row("dec"), AF.Exp),
            reads=("rows",), writes=("rows",), tag="dexp")
        vop(lambda nc, eng: nc.vector.tensor_copy(sscale[:], row("dexp")[:, C - 1::C]),
            reads=("rows",), writes=("sscale",), tag="sscale")
        for h in range(H):
            sdma(c_selrow[:], dr["c_selrow"][:, h * 128:(h + 1) * 128],
                 reads=(), writes=("c_selrow",), tag="ld_selrow")
            mm(pk[5][:, 0:NCH], c_selrow[:, :], sscale[:],
               start=True, stop=True, reads=("c_selrow", "sscale"),
               writes=(PKR[5],), tag="sscmm")
            sop(lambda nc, eng, h_=h: nc.scalar.copy(
                ssc[:, h_ * NCH:(h_ + 1) * NCH], pk[5][:, 0:NCH]),
                reads=(PKR[5],), writes=("ssc",), tag="sscev")

        for kc in range(KC):
            sdma(c_headsel[:], dr["c_headsel"][:, kc * H:(kc + 1) * H],
                 reads=(), writes=("c_headsel",), tag="ld_hsel")
            sop(lambda nc, eng, kc_=kc: nc.scalar.activation(
                tmp_sq[:, 0:T], ln1_rhs(kc_, 0, T), AF.Square),
                reads=("ln1",), writes=("sqtmp",), tag="ksq")
            for t in range(NTS):
                mm(pk[4 + t][0:H, 0:TS], c_headsel[:, 0:H],
                   tmp_sq[:, t * TS:(t + 1) * TS],
                   start=(kc == 0), stop=(kc == KC - 1),
                   reads=("c_headsel", "sqtmp"), writes=(PKR[4 + t],), tag="ssqmm")
        for t in range(NTS):
            sop(lambda nc, eng, t_=t: nc.scalar.activation(
                row("nrm", t_ * TS, (t_ + 1) * TS), pk[4 + t_][0:H, 0:TS], AF.Sqrt),
                reads=(PKR[4 + t],), writes=("rows",), tag="nrm")
        vop(lambda nc, eng: nc.vector.tensor_scalar_max(row("nrm"), row("nrm"), 1e-12),
            reads=("rows",), writes=("rows",), tag="nrmmax")
        vop(lambda nc, eng: nc.vector.reciprocal(row("nrm"), row("nrm")),
            reads=("rows",), writes=("rows",), tag="rnorm")
        sop(lambda nc, eng: nc.scalar.activation(t1(), row("nrm"), AF.Ln),
            reads=("rows",), writes=("rowsB",), tag="lnr")
        vop(lambda nc, eng: nc.vector.memset(row("rwsh", 0, 1), 0.0),
            reads=(), writes=("rows",))
        vop(lambda nc, eng: nc.vector.tensor_copy(
            row("rwsh", 1, T), row("nrm", 0, T - 1)),
            reads=("rows",), writes=("rows",), tag="rwsh")
        vop(lambda nc, eng: nc.vector.memset(t2(0, 1), NEG),
            reads=(), writes=("rowsB",))
        vop(lambda nc, eng: nc.vector.tensor_copy(t2(1, T), t1(0, T - 1)),
            reads=("rowsB",), writes=("rowsB",), tag="lnrwsh")
        vop(lambda nc, eng: nc.vector.tensor_sub(
            srow(0, p=H), row("dec"), t2()),
            reads=("rows", "rowsB"), writes=("stats",), tag="pld1")
        vop(lambda nc, eng: nc.vector.tensor_add(
            row("lnbeta"), row("dec"), row("lnbeta")),
            reads=("rows",), writes=("rows",), tag="pld2a")
        vop(lambda nc, eng: nc.vector.tensor_add(
            srow(1, p=H), row("lnbeta"), t2()),
            reads=("rows", "rowsB"), writes=("stats",), tag="pld2")
        vop(lambda nc, eng: nc.vector.tensor_add(
            srow(2, p=H), row("dec"), t1()),
            reads=("rows", "rowsB"), writes=("stats",), tag="pld3")
        vop(lambda nc, eng: nc.vector.tensor_mul(t3(), row("gate"), row("dexp")),
            reads=("rows",), writes=("rowsB",), tag="ogr_a")
        vop(lambda nc, eng: nc.vector.tensor_mul(t3(), t3(), row("nrm")),
            reads=("rows", "rowsB"), writes=("rowsB",), tag="ogr")
        vop(lambda nc, eng: nc.vector.scalar_tensor_tensor(
            row("lnbeta"), row("beta"), -1.0, row("rwsh"), ALU.mult, ALU.mult),
            reads=("rows",), writes=("rows",), tag="negbrw")
        vop(lambda nc, eng: nc.vector.tensor_mul(
            row("dexp"), row("lnbeta"), row("dexp")),
            reads=("rows",), writes=("rows",), tag="negbdrw")
        vop(lambda nc, eng: nc.vector.tensor_mul(
            row("sufexp"), row("sufexp"), row("rwsh")),
            reads=("rows",), writes=("rows",), tag="bw")

        ddump("rows", b0[0:8, :], ("rows",))
        ddump("pld1", srow(0, 0, T, p=H), ("stats",))
        ddump("pld2", srow(1, 0, T, p=H), ("stats",))
        ddump("pld3", srow(2, 0, T, p=H), ("stats",))
        ddump("vT", b1[:, 0:8 * T], ("vT",))
        ddump("sscv", ssc[:, :], ("ssc",))
        TSRC = [lambda n: row("dec", n * C, (n + 1) * C),
                lambda n: srow(0, n * C, (n + 1) * C, p=H),
                lambda n: row("lnbeta", n * C, (n + 1) * C),
                lambda n: row("dexp", n * C, (n + 1) * C),
                lambda n: row("gate", n * C, (n + 1) * C),
                lambda n: row("beta", n * C, (n + 1) * C),
                lambda n: t3(n * C, (n + 1) * C),
                lambda n: row("sufexp", n * C, (n + 1) * C)]
        TSRC_R = ["rows", "stats", "rows", "rows", "rows", "rows", "rowsB", "rows"]
        def tscl_col(h, n, s):
            if s < 5:
                i = (n * 5 + s) * 8 + h
                return tscl32[:, i:i + 1]
            i = (n * 3 + (s - 5)) * 8 + h
            return tscl16[:, i:i + 1]
        def tscl_colb(h, n, s):
            return tscl_col(h, n, s)[:, 0:1].broadcast_to([128, 128])
        ltm_ = tscl32[:, 0:NTT * 4]
        mtm_ = tscl32[:, NTT * 4:NTT * 8 + 1]
        for n in range(NCH):
            for s in range(8):
                trans(pk[3][:, s * 8:(s + 1) * 8], TSRC[s](n), c_ident[0:H, 0:H],
                      reads=(TSRC_R[s], "c_ident"), writes=(PKR[3],), tag="tsclT")
            sop(lambda nc, eng, n_=n: nc.scalar.copy(
                tscl32[:, n_ * 40:(n_ + 1) * 40], pk[3][:, 0:40]),
                reads=(PKR[3],), writes=("tscl",), tag="tscl_ev32")
            sop(lambda nc, eng, n_=n: nc.scalar.copy(
                tscl16[:, n_ * 24:(n_ + 1) * 24], pk[3][:, 40:64]),
                reads=(PKR[3],), writes=("tscl",), tag="tscl_ev16")

        ddump("tscl32", tscl32[:, :], ("tscl",))
        ddump("tscl16", tscl16[:, :], ("tscl",))
        for h in range(H):
            sdma(c_selrow[:], dr["c_selrow"][:, h * 128:(h + 1) * 128],
                 reads=(), writes=("c_selrow",), tag="ld_selrowh")
            vop(lambda nc, eng: nc.vector.memset(Ssta[:], 0.0),
                reads=(), writes=("S",), tag="S0")
            vop(lambda nc, eng: nc.vector.memset(Sb[:], 0.0),
                reads=(), writes=("Sbfr",), tag="Sb0")
            for n in range(NCH):
                tsl, ns = n // 4, n % 4
                par = (h * NCH + n) % 2
                pG, pGr = (pk[1], PKR[1]) if par == 0 else (pk[5], PKR[5])
                pT, pTr = (pk[3], PKR[3]) if par == 0 else (pk[6], PKR[6])
                QS = 2 * par
                if ns == 0:
                    lo = tsl * 512
                    wid = min(512, T - lo)
                    nch_in = wid // C
                    for fam, pldrow, moff, colslot in (
                            (1, 0, 0, 0), (2, 1, 128, 1), (3, 2, 256, 1)):
                        for q in range(wid // C):
                            mm(pk[0][:, q * C:(q + 1) * C], c_identb[:, :],
                               c_maskb[:, moff:moff + C],
                               start=(q == 0), stop=False,
                               reads=("c_identb", "c_maskb"), writes=(PKR[0],),
                               tag="Lmask")
                        mm(pk[0][:, 0:wid], c_selrow[:, :],
                           srow(pldrow, lo, lo + wid, p=H),
                           start=False, stop=True,
                           reads=("c_selrow", "stats"), writes=(PKR[0],),
                           tag="Lpld")
                        i0 = ((tsl * 4) * 5 + colslot) * 8 + h
                        col = tscl32[:, i0: i0 + (nch_in - 1) * 40 + 1: 40]
                        colb = col[:, :, None].broadcast_to([128, nch_in, C])
                        if fam == 1:
                            vop(lambda nc, eng, d=colb, f=fam, w=wid:
                                nc.vector.tensor_tensor(
                                    Lsl(f, 0, w).rearrange("p (a b) -> p a b", b=C),
                                    d,
                                    pk[0][:, 0:w].rearrange("p (a b) -> p a b", b=C),
                                    ALU.subtract),
                                reads=("tscl", PKR[0]), writes=("Lz",), tag="Lsub")
                        else:
                            vop(lambda nc, eng, d=colb, f=fam, w=wid:
                                nc.vector.tensor_tensor(
                                    Lsl(f, 0, w).rearrange("p (a b) -> p a b", b=C),
                                    pk[0][:, 0:w].rearrange("p (a b) -> p a b", b=C),
                                    d, ALU.subtract),
                                reads=("tscl", PKR[0]), writes=("Lz",), tag="Lsub")
                        sop(lambda nc, eng, f=fam, w=wid: nc.scalar.activation(
                            Lsl(f, 0, w), Lsl(f, 0, w), AF.Exp),
                            reads=("Lz",), writes=("Lz",), tag="Lexp")
                a = n * C
                mm(pG[:, 0:128], kw_sl(h, a, a + C), kw_sl(h, a, a + C),
                   True, True, reads=("ln1",), writes=(pGr,), tag="G")
                mm(pG[:, 128:256], kw_sl(h, a, a + C), k_sl(h, a, a + C),
                   True, True, reads=("ln1",), writes=(pGr,), tag="attnT")
                trans(pT[:, 0:64].bitcast(BF16), vT_sl(h, a, a + C), c_identb[:],
                      reads=("vT", "c_identb"), writes=(pTr,), tag="trV")
                trans(pT[:, 64:128].bitcast(BF16), kw_sl(h, a, a + C), c_identb[:],
                      reads=("ln1", "c_identb"), writes=(pTr,), tag="trKw")
                mm(pk[2][:, 0:128], kw_sl(h, a, a + C), Sb[:],
                   True, True, reads=("ln1", "Sbfr"), writes=(PKR[2],), tag="KwS")
                mm(pk[2][:, 128:256], k_sl(h, a, a + C), Sb[:],
                   True, True, reads=("ln1", "Sbfr"), writes=(PKR[2],), tag="KS")
                Lq = lambda f, ns_=ns: Lsl(f, ns_ * C, ns_ * C + C)
                ot_extra = ("sqtmp", "rowsB") if (h == 0 and n == 0) else ()
                vop(lambda nc, eng, c1=tscl_col(h, n, 2), L1q=Lq(1), s=QS, pG_=pG:
                    nc.vector.scalar_tensor_tensor(
                        pbt[:, s], pG_[:, 0:128], c1, L1q, ALU.mult, ALU.mult),
                    reads=(pGr, "tscl", "Lz"), writes=("P%d" % QS,), tag="negM")
                vop(lambda nc, eng, L2q=Lq(2), s=QS, pG_=pG:
                    nc.vector.scalar_tensor_tensor(
                        qbt[:, s], pG_[:, 0:128], -1.0, L2q, ALU.mult, ALU.mult),
                    reads=(pGr, "Lz"), writes=("Q%d" % QS,), tag="negMT")
                vop(lambda nc, eng, s=QS: nc.vector.tensor_add(
                    qibt[:, s], qbt[:, s], c_identb[:]),
                    reads=("Q%d" % QS, "c_identb"), writes=("QI%d" % QS,), tag="QI0")
                vop(lambda nc, eng, L3q=Lq(3), pG_=pG: nc.vector.tensor_mul(
                    attm[:], pG_[:, 128:256], L3q),
                    reads=(pGr, "Lz"), writes=("attm",), tag="attnTm")
                vop(lambda nc, eng, c1=tscl_colb(h, n, 5), pT_=pT:
                    nc.vector.tensor_tensor(u1t[:], pT_[:, 0:64].bitcast(BF16),
                                            c1, ALU.mult),
                    reads=(pTr, "tscl"), writes=("u1t",), tag="u1")
                vop(lambda nc, eng, c1=tscl_col(h, n, 3):
                    nc.vector.scalar_tensor_tensor(
                        vvt[:, 0], pk[2][:, 0:128], c1, u1t[:], ALU.mult, ALU.add),
                    reads=(PKR[2], "tscl", "u1t"), writes=("vv0",), tag="U")
                vop(lambda nc, eng, c1=tscl_colb(h, n, 7), pT_=pT:
                    nc.vector.tensor_tensor(kwdw[:], pT_[:, 64:128].bitcast(BF16),
                                            c1, ALU.mult),
                    reads=(pTr, "tscl"), writes=("kwdw",), tag="kwdw")
                for m in range(NINV):
                    cur, nxt = m % 2, 1 - m % 2
                    mm(pk[4][:, 0:128], qibt[:, QS + cur], vvt[:, cur],
                       True, True, reads=("QI%d" % (QS + cur), "vv%d" % cur),
                       writes=(PKR[4],), tag="apply")
                    vop(lambda nc, eng, nx=nxt: nc.vector.tensor_copy(
                        vvt[:, nx], pk[4][:, 0:128]),
                        reads=(PKR[4],), writes=("vv%d" % nxt,), tag="vv_ev")
                    if m < NINV - 1:
                        mm(pT[:, 128:256], pbt[:, QS + cur], qbt[:, QS + cur],
                           True, True,
                           reads=("P%d" % (QS + cur), "Q%d" % (QS + cur)),
                           writes=(pTr,), tag="Qsq")
                        vop(lambda nc, eng, nx=nxt, pT_=pT, s=QS:
                            nc.vector.tensor_add(
                                qibt[:, s + nx], pT_[:, 128:256], c_identb[:]),
                            reads=(pTr, "c_identb"), writes=("QI%d" % (QS + nxt),),
                            tag="QIev")
                        if m < NINV - 2:
                            mm(pT[:, 256:384], qbt[:, QS + cur], pbt[:, QS + cur],
                               True, True,
                               reads=("P%d" % (QS + cur), "Q%d" % (QS + cur)),
                               writes=(pTr,), tag="Psq")
                            sop(lambda nc, eng, nx=nxt, pT_=pT, s=QS:
                                nc.scalar.copy(qbt[:, s + nx], pT_[:, 128:256]),
                                reads=(pTr,), writes=("Q%d" % (QS + nxt),), tag="Qev")
                            sop(lambda nc, eng, nx=nxt, pT_=pT, s=QS:
                                nc.scalar.copy(pbt[:, s + nx], pT_[:, 256:384]),
                                reads=(pTr,), writes=("P%d" % (QS + nxt),), tag="Pev")
                vfin = "vv%d" % (NINV % 2)
                vfin_ap = vvt[:, NINV % 2]
                mm(pk[7][:, 0:128], attm[:], vfin_ap, True, True,
                   reads=("attm", vfin), writes=(PKR[7],), tag="o2")
                vop(lambda nc, eng, c1=tscl_colb(h, n, 6): nc.vector.tensor_tensor(
                    o1t[:], pk[2][:, 128:256], c1, ALU.mult),
                    reads=(PKR[2], "tscl"), writes=("o1t",), tag="o1")
                vop(lambda nc, eng, c1=tscl_col(h, n, 4):
                    nc.vector.scalar_tensor_tensor(
                        otm[:], pk[7][:, 0:128], c1, o1t[:], ALU.mult, ALU.add),
                    reads=(PKR[7], "tscl", "o1t"), writes=("otm",), tag="otm")
                trans(pk[7][:, 128:192].bitcast(BF16), otm[:], c_identb[:],
                      reads=("otm", "c_identb"), writes=(PKR[7],), tag="trO")
                sop(lambda nc, eng, h_=h, a_=a: nc.scalar.copy(
                    oT_sl(h_, a_, a_ + C), pk[7][:, 128:192].bitcast(BF16)),
                    reads=(PKR[7],), writes=("oT",) + ot_extra, tag="oT_ev")
                mm(pk[7][:, 256:384], kwdw[:], vfin_ap, True, True,
                   reads=("kwdw", vfin), writes=(PKR[7],), tag="B")
                vop(lambda nc, eng, h_=h, n_=n:
                    nc.vector.scalar_tensor_tensor(
                        Ssta[:], Ssta[:], ssc[:, h_ * NCH + n_: h_ * NCH + n_ + 1],
                        pk[7][:, 256:384], ALU.mult, ALU.add),
                    reads=("S", "ssc", PKR[7]), writes=("S",), tag="Supd")
                sop(lambda nc, eng: nc.scalar.copy(Sb[:], Ssta[:]),
                    reads=("S",), writes=("Sbfr",), tag="Sbf")

        sdma(b0.rearrange("p (c t) -> p c t", c=8), dr["xT"].rearrange("(c p) t -> p c t", p=128),
             reads=(), writes=("x2", "rows", "xT"), tag="ld_x2")
        for jc in range(KC):
            buf = 0
            gdma(wst[:, buf], dr["woT"].rearrange("(k p) j -> p k j", p=128)
                 [:, :, jc * 128:(jc + 1) * 128],
                 reads=(), writes=("ws%d" % buf,), tag="ld_wo")
            for t in range(NTS):
                pb, pbr = pk[(jc * NTS + t) % 2], PKR[(jc * NTS + t) % 2]
                for kc in range(KC):
                    mm(pb[:, 0:TS], wst[:, buf, kc, :],
                       oT_sl(kc, t * TS, (t + 1) * TS),
                       start=(kc == 0), stop=(kc == KC - 1),
                       reads=("ws%d" % buf, "oT"), writes=(pbr,), tag="oproj")
                vop(lambda nc, eng, jc_=jc, t_=t, pb_=pb:
                    nc.vector.tensor_add(
                        x2_sl(jc_, t_ * TS, (t_ + 1) * TS),
                        pb_[:, 0:TS], x2_sl(jc_, t_ * TS, (t_ + 1) * TS)),
                    reads=(pbr, "x2"), writes=("x2",), tag="res1")
                vop(lambda nc, eng, jc_=jc, t_=t:
                    nc.vector.tensor_add(
                        x2_sl(jc_, t_ * TS, (t_ + 1) * TS),
                        ln1_rhs(jc_, t_ * TS, (t_ + 1) * TS),
                        x2_sl(jc_, t_ * TS, (t_ + 1) * TS)),
                    reads=("ln1", "x2"), writes=("x2",), tag="res1b")

        ddump("x2a", b0[:, 0:8 * T], ("x2",))
        emit_ln("x2", x2_sl, "ln1", subkey=True, pad=False)
        def ln2_rhs(kc, lo, hi):
            return b2[:, kc, 1 + lo: 1 + hi]

        for kc in range(KC):
            mm(pk[6][0:1, 0:4], c_meanones[:, 0:1], wr32[:, kc, :],
               start=(kc == 0), stop=(kc == KC - 1),
               reads=("c_meanones", "wr32"), writes=(PKR[6],), tag="colsum")
        vop(lambda nc, eng: nc.vector.tensor_scalar_mul(
            negcs[:], pk[6][0:1, 0:4], -float(D)),
            reads=(PKR[6],), writes=("negcs",), tag="negcs")
        for tt in range(NTT):
            pb, pbr = pk[2 + tt % 2], PKR[2 + tt % 2]
            for kc in range(KC):
                mm(pb[0:128, 0:4],
                   x2_sl(kc, tt * 128, (tt + 1) * 128),
                   wr32[:, kc, :], start=(kc == 0), stop=False,
                   reads=("x2", "wr32"), writes=(pbr,), tag="router")
            mm(pb[0:128, 0:4], srow(0, tt * 128, (tt + 1) * 128),
               negcs[:], start=False, stop=True,
               reads=("stats", "negcs"), writes=(pbr,), tag="router_m")
            trans(pk[4 + tt % 2][:, 0:1], srow(1, tt * 128, (tt + 1) * 128),
                  c_ident[0:1, 0:1],
                  reads=("stats", "c_ident"), writes=(PKR[4 + tt % 2],),
                  tag="trRstd")
            sop(lambda nc, eng, tt_=tt: nc.scalar.copy(
                rstdc[:], pk[4 + tt_ % 2][:, 0:1]),
                reads=(PKR[4 + tt % 2],), writes=("rstdc",), tag="rstdev")
            vop(lambda nc, eng, tt_=tt, pb_=pb: nc.vector.scalar_tensor_tensor(
                ltm_[:, tt_ * 4:(tt_ + 1) * 4], pb_[0:128, 0:4], rstdc[:, 0:1],
                brr_sb[:], ALU.mult, ALU.add),
                reads=(pbr, "rstdc", "brr"),
                writes=("ltm",) + (("tscl",) if tt == 0 else ()), tag="logits")
        for tt in range(NTT):
            base = tt * 4
            for e in range(4):
                first = True
                for j in range(4):
                    if j == e:
                        continue
                    opc = ALU.is_gt if j < e else ALU.is_ge
                    if first:
                        vop(lambda nc, eng, b_=base, e_=e, j_=j, op_=opc:
                            nc.vector.tensor_tensor(
                                mtm_[:, b_ + e_:b_ + e_ + 1],
                                ltm_[:, b_ + e_:b_ + e_ + 1],
                                ltm_[:, b_ + j_:b_ + j_ + 1], op_),
                            reads=("ltm",), writes=("mtm",), tag="cmp")
                        first = False
                    else:
                        vop(lambda nc, eng, b_=base, e_=e, j_=j, op_=opc:
                            nc.vector.tensor_tensor(
                                mtm_[:, NTT * 4:NTT * 4 + 1],
                                ltm_[:, b_ + e_:b_ + e_ + 1],
                                ltm_[:, b_ + j_:b_ + j_ + 1], op_),
                            reads=("ltm",), writes=("mtmt",), tag="cmp")
                        vop(lambda nc, eng, b_=base, e_=e:
                            nc.vector.tensor_mul(
                                mtm_[:, b_ + e_:b_ + e_ + 1],
                                mtm_[:, b_ + e_:b_ + e_ + 1],
                                mtm_[:, NTT * 4:NTT * 4 + 1]),
                            reads=("mtm", "mtmt"), writes=("mtm",), tag="cmpm")
            trans(pk[4][0:4, 0:128], mtm_[:, base:base + 4], c_ident[:],
                  reads=("mtm", "c_ident"), writes=(PKR[4],), tag="trM")
            sop(lambda nc, eng, tt_=tt: nc.scalar.copy(
                maskfm[:, tt_ * 128:(tt_ + 1) * 128], pk[4][0:4, 0:128]),
                reads=(PKR[4],),
                writes=("maskfm", "mfmt%d" % (tt // 4)) +
                       (("Lz",) if tt == 0 else ()), tag="maskev")

        ddump("ltm", ltm_[:, :], ("ltm",))
        ddump("maskfm", maskfm[:, :], ("maskfm",))
        for e in range(4):
            sdma(be1c_sb[:], dr["be1c"][:, e * 32:(e + 1) * 32],
                 reads=(), writes=("be1c",), tag="ld_be1c")
            gdma(be2e_sb[:], dr["be2e"].rearrange("e (f j) -> e f j", f=4)[:, e],
                 reads=(), writes=("be2e",) + (("Lz",) if e == 0 else ()),
                 tag="ld_be2e")
            gdma(sel4_sb[:], dr["c_sel4"].rearrange("e (f j) -> e f j", f=4)[:, e],
                 reads=(), writes=("sel4",), tag="ld_sel4")
            for t in range(NTS):
                for fcb in range(8):
                    buf = fcb % 2
                    first_ld = (e == 0 and t == 0 and fcb < 2)
                    gdma(we1b(buf),
                         dr["we1"][e].rearrange("(k p) f -> p k f", p=128)
                         [:, :, fcb * 512:(fcb + 1) * 512],
                         reads=(),
                         writes=("we1b%d" % buf,) + (("oT", "sqtmp", "rowsB") if first_ld else ()),
                         tag="ld_we1")
                    for fcl in range(4):
                        fc = fcb * 4 + fcl
                        pb, pbr = pk[fc % 2], PKR[fc % 2]
                        for kc in range(KC):
                            mm(pb[:, 0:TS], we1b(buf)[:, kc, fcl * 128:(fcl + 1) * 128],
                               ln2_rhs(kc, t * TS, (t + 1) * TS),
                               start=(kc == 0), stop=(kc == KC - 1),
                               reads=("we1b%d" % buf, "ln1t%d" % t),
                               writes=(pbr,), tag="h1mm")
                        h1_extra = ("vT",) if (e == 0 and t == 0 and fc == 0) else ()
                        sop(lambda nc, eng, fc_=fc, e_=e, pb_=pb:
                            nc.scalar.activation(
                                h1m_sl(fc_, 0, TS), pb_[:, 0:TS], AF.Relu,
                                bias=be1c_sb[:, fc_:fc_ + 1]),
                            reads=(pbr, "be1c"), writes=("h1m",) + h1_extra,
                            tag="relu")
                if dbg and t == 0 and e < 2:
                    ddump("h1m_e%d" % e, b1[:, 0:32 * TS], ("h1m",))
                mm(pk[0][:, 0:TS], sel4_sb,
                   maskfm[:, t * TS:(t + 1) * TS], start=True, stop=True,
                   reads=("sel4", "mfmt%d" % t), writes=(PKR[0],), tag="mrepmm")
                sop(lambda nc, eng: nc.scalar.copy(mrep[:, 0:TS], pk[0][:, 0:TS]),
                    reads=(PKR[0],), writes=("mrep",), tag="mrepev")
                for fcb in range(8):
                    buf = fcb % 2
                    first_ld = (e == 0 and t == 0 and fcb < 2)
                    gdma(we2b(buf),
                         dr["we2"][e].rearrange("(f p) j -> p f j", p=128)
                         [:, fcb * 4:(fcb + 1) * 4, :],
                         reads=(),
                         writes=("we2b%d" % buf,) + (("oT", "rowsB") if first_ld else ()),
                         tag="ld_we2")
                    for jc in range(KC):
                        if fcb == 0:
                            mm(pk[jc][:, 0:TS], be2e_sb[:, jc * 128:(jc + 1) * 128],
                               maskfm[:, t * TS:(t + 1) * TS],
                               start=True, stop=False,
                               reads=("be2e", "mfmt%d" % t), writes=(PKR[jc],),
                               tag="oebias")
                        for fcl in range(4):
                            fc = fcb * 4 + fcl
                            mm(pk[jc][:, 0:TS],
                               we2b(buf)[:, fcl, jc * 128:(jc + 1) * 128],
                               h1m_sl(fc, 0, TS),
                               start=False, stop=(fcb == 7 and fcl == 3),
                               reads=("we2b%d" % buf, "h1m"), writes=(PKR[jc],),
                               tag="oemm")
                for jc in range(KC):
                    vop(lambda nc, eng, jc_=jc: nc.vector.tensor_mul(
                        tmpF[:, 0:TS], pk[jc_][:, 0:TS], mrep[:, 0:TS]),
                        reads=(PKR[jc], "mrep"), writes=("stats",), tag="oemask")
                    vop(lambda nc, eng, jc_=jc, t_=t: nc.vector.tensor_add(
                        x2_sl(jc_, t_ * TS, (t_ + 1) * TS),
                        tmpF[:, 0:TS], x2_sl(jc_, t_ * TS, (t_ + 1) * TS)),
                        reads=("stats", "x2"), writes=("x2",), tag="oeev")
                if dbg and t == 0 and e == 0:
                    ddump("x2e0", b0[:, 0:8 * T], ("x2",))

        sdma(out_d.rearrange("(c p) t -> p c t", p=128), b0.rearrange("p (c t) -> p c t", c=8),
             reads=("x2",), writes=("outd",), tag="st_out")

        dma_keys = em.finalize()
        for k in dma_keys:
            sems[k] = _st.enter_context(nc.semaphore(k))
        block = _st.enter_context(nc.Block())
        block.sync(lambda eng: em.run_engine(nc, eng, "sync", sems))
        block.gpsimd(lambda eng: em.run_engine(nc, eng, "gpsimd", sems))
        block.tensor(lambda eng: em.run_engine(nc, eng, "tensor", sems))
        block.vector(lambda eng: em.run_engine(nc, eng, "vector", sems))
        block.scalar(lambda eng: em.run_engine(nc, eng, "scalar", sems))

    return nc


def make_in_maps(inputs, T, ncores):
    shared = host_shared(inputs)
    consts = host_constants()
    x = np.asarray(inputs["L0_attn_x"])
    in_maps = []
    for b in range(ncores):
        m = {}
        for k, v in consts.items():
            m[k] = np.ascontiguousarray(v, dtype=np.float32)
        for k, v in shared.items():
            m[k] = np.ascontiguousarray(v, dtype=np.float32)
        m["xT"] = np.ascontiguousarray(x[b, :T, :].T, dtype=np.float32)
        in_maps.append(m)
    return in_maps


def kernel(**inputs):
    T = 2048
    nc = build_nc(T)
    in_maps = make_in_maps(inputs, T, 8)
    res = run_bass_kernel_spmd(nc, in_maps, core_ids=list(range(8)))
    out = np.stack([np.asarray(r["out"]).T for r in res.results], axis=0)
    return np.ascontiguousarray(out, dtype=np.float32)


# version 16
# speedup vs baseline: 1.0292x; 1.0292x over previous
import numpy as np
import concourse.bass as bass
import concourse.mybir as mybir
from concourse.bass_utils import run_bass_kernel_spmd

F32 = mybir.dt.float32
BF16 = mybir.dt.bfloat16
AF = mybir.ActivationFunctionType
ALU = mybir.AluOpType

D = 1024
H = 8
C = 128
KC = 8
NEG = -1.0e30
BIG = float(2.0 ** 30)
NINV = 7
TF = 2048

ENGS = ("sync", "gpsimd", "tensor", "vector", "scalar")


class Em:

    def __init__(self):
        self.ops = []
        self.lw = {}
        self.rd = {}

    def op(self, eng, fn, reads=(), writes=(), dma=False, tag=""):
        deps = set()
        for r in reads:
            if r in self.lw:
                deps.add(self.lw[r])
        for w in writes:
            if w in self.lw:
                deps.add(self.lw[w])
            deps.update(self.rd.get(w, ()))
        i = len(self.ops)
        assert not dma or len(writes) >= 1
        self.ops.append({"i": i, "eng": eng, "fn": fn, "deps": deps,
                         "dma": dma, "tag": tag,
                         "key": ("d_" + writes[0]) if dma else eng})
        for r in reads:
            self.rd.setdefault(r, set()).add(i)
        for w in writes:
            self.lw[w] = i
            self.rd[w] = set()
        return i

    def finalize(self):
        n = len(self.ops)
        self.needs_inc = [False] * n
        for o in self.ops:
            if o["dma"]:
                self.needs_inc[o["i"]] = True
        for o in self.ops:
            for d in o["deps"]:
                if self.ops[d]["eng"] != o["eng"] or self.ops[d]["dma"]:
                    self.needs_inc[d] = True
        self.val = {}
        cnt = {}
        for o in self.ops:
            if self.needs_inc[o["i"]]:
                k = o["key"]
                cnt[k] = cnt.get(k, 0) + (16 if o["dma"] else 1)
                self.val[o["i"]] = cnt[k]
        self.final = cnt
        self.dma_keys = {}
        for o in self.ops:
            if o["dma"]:
                self.dma_keys[o["key"]] = o["eng"]
        return sorted(self.dma_keys)

    def run_engine(self, nc, engobj, ename, sems):
        waited = {k: 0 for k in sems}
        drained_upto = -1
        last_own = -1
        for o in self.ops:
            if o["eng"] != ename:
                continue
            need_drain = False
            wneed = {}
            for d in o["deps"]:
                dd = self.ops[d]
                if dd["eng"] == ename and not dd["dma"]:
                    if ename != "tensor" and d > drained_upto:
                        need_drain = True
                else:
                    k = dd["key"]
                    v = self.val[d]
                    if v > waited[k]:
                        wneed[k] = max(wneed.get(k, 0), v)
            if need_drain:
                engobj.drain()
                drained_upto = last_own
            for k, v in sorted(wneed.items()):
                engobj.wait_ge(sems[k], v)
                waited[k] = v
            inst = o["fn"](nc, engobj)
            if self.needs_inc[o["i"]]:
                inst.then_inc(sems[o["key"]], 16 if o["dma"] else 1)
            last_own = o["i"]
        for k, e in sorted(self.dma_keys.items()):
            if e == ename and self.final.get(k, 0) > waited[k]:
                engobj.wait_ge(sems[k], self.final[k])


def host_constants():
    c = {}
    c["c_ident"] = np.eye(128, dtype=np.float32)
    c["c_meanones"] = np.full((128, 1), 1.0 / D, dtype=np.float32)
    hs = np.zeros((128, KC * H), dtype=np.float32)
    for kc in range(KC):
        hs[:, kc * H + kc] = 1.0
    c["c_headsel"] = hs
    sel = np.zeros((H, H * 128), dtype=np.float32)
    for h in range(H):
        sel[h, h * 128:(h + 1) * 128] = 1.0
    c["c_selrow"] = sel
    ii = np.arange(C)
    m1 = np.where(ii[:, None] > ii[None, :], 0.0, BIG).astype(np.float32)
    m2 = np.where(ii[None, :] > ii[:, None], 0.0, -BIG).astype(np.float32)
    m3 = np.where(ii[None, :] >= ii[:, None], 0.0, -BIG).astype(np.float32)
    c["c_maskb"] = np.concatenate([m1, m2, m3], axis=1)
    sel4 = np.zeros((4, 4 * 128), dtype=np.float32)
    for e in range(4):
        sel4[e, e * 128:(e + 1) * 128] = 1.0
    c["c_sel4"] = sel4
    return c


def host_shared(inputs):
    d = {}
    d["wwT"] = np.ascontiguousarray(np.asarray(inputs["L0_attn_W_write"]).T)
    d["woT"] = np.ascontiguousarray(np.asarray(inputs["L0_attn_W_out"]).T)
    d["wgbaT"] = np.ascontiguousarray(np.concatenate(
        [np.asarray(inputs["L0_attn_W_gate"]),
         np.asarray(inputs["L0_attn_W_beta"]),
         np.asarray(inputs["L0_attn_W_alpha"])], axis=0).T)
    d["dtb"] = np.ascontiguousarray(np.asarray(inputs["L0_attn_dt_bias"]).reshape(H, 1))
    d["alg"] = np.ascontiguousarray(np.asarray(inputs["L0_attn_A_log"]).reshape(H, 1))
    d["wr"] = np.ascontiguousarray(np.asarray(inputs["L0_ffn_Wr"]))
    d["brr"] = np.ascontiguousarray(np.broadcast_to(
        np.asarray(inputs["L0_ffn_br"]).reshape(1, 4), (128, 4)))
    d["we1"] = np.ascontiguousarray(np.asarray(inputs["L0_ffn_We1"]))
    be1 = np.asarray(inputs["L0_ffn_be1"])
    d["be1c"] = np.ascontiguousarray(
        be1.reshape(4, 32, 128).transpose(2, 0, 1).reshape(128, 128))
    d["we2"] = np.ascontiguousarray(np.asarray(inputs["L0_ffn_We2"]))
    be2 = np.asarray(inputs["L0_ffn_be2"])
    be2e = np.zeros((4, 4, 1024), dtype=np.float32)
    for e in range(4):
        be2e[e, e, :] = be2[e]
    d["be2e"] = be2e.reshape(4, 4096)
    return d


def build_nc(T, dbg=False):
    NCH = T // C
    NTS = max(T // 512, 1)
    TS = min(512, T)
    NTT = T // 128
    nc = bass.Bass("TRN2", target_bir_lowering=False)

    dr = {}
    def din(name, shape):
        dr[name] = nc.dram_tensor(name, list(shape), F32, kind="ExternalInput").ap()
    din("xT", (D, T))
    din("wwT", (D, D)); din("woT", (D, D)); din("wgbaT", (D, 24))
    din("dtb", (H, 1)); din("alg", (H, 1))
    din("wr", (D, 4)); din("brr", (128, 4))
    din("we1", (4, D, 4096)); din("be1c", (128, 128))
    din("we2", (4, 4096, D)); din("be2e", (4, 4096))
    for k, v in host_constants().items():
        din(k, v.shape)
    out_d = nc.dram_tensor("out", [D, T], F32, kind="ExternalOutput").ap()

    em = Em()

    from contextlib import ExitStack
    with ExitStack() as _st:
        big0 = _st.enter_context(nc.sbuf_tensor("big0", [128, 8 * T], F32))
        big1 = _st.enter_context(nc.sbuf_tensor("big1", [128, 16384], BF16))
        big2 = _st.enter_context(nc.sbuf_tensor("big2", [128, KC, T + 1], BF16))
        big3 = _st.enter_context(nc.sbuf_tensor("big3", [128, 32768], BF16))
        wstr = _st.enter_context(nc.sbuf_tensor("wstr", [128, 1, KC, 128], BF16))
        tscl32 = _st.enter_context(nc.sbuf_tensor("tscl32", [128, NCH * 40], F32))
        tscl16 = _st.enter_context(nc.sbuf_tensor("tscl16", [128, NCH * 24], BF16))
        ssc = _st.enter_context(nc.sbuf_tensor("ssc", [128, H * NCH], F32))
        bndt = _st.enter_context(nc.sbuf_tensor("bndt", [H, NCH], F32))
        sscale = _st.enter_context(nc.sbuf_tensor("sscale", [H, NCH], F32))
        expa = _st.enter_context(nc.sbuf_tensor("expa", [H, 1], F32))
        ndtb = _st.enter_context(nc.sbuf_tensor("ndtb", [H, 1], F32))
        dtb_sb = _st.enter_context(nc.sbuf_tensor("dtb_sb", [H, 1], F32))
        alg_sb = _st.enter_context(nc.sbuf_tensor("alg_sb", [H, 1], F32))
        Ssta = _st.enter_context(nc.sbuf_tensor("Ssta", [128, 128], F32))
        Sb = _st.enter_context(nc.sbuf_tensor("Sbf", [128, 128], BF16))
        u1t = _st.enter_context(nc.sbuf_tensor("u1t", [128, 128], BF16))
        o1t = _st.enter_context(nc.sbuf_tensor("o1t", [128, 128], BF16))
        otm = _st.enter_context(nc.sbuf_tensor("otm", [128, 128], BF16))
        attm = _st.enter_context(nc.sbuf_tensor("attm", [128, 128], BF16))
        kwdw = _st.enter_context(nc.sbuf_tensor("kwdw", [128, 128], BF16))
        vvt = _st.enter_context(nc.sbuf_tensor("vvt", [128, 2, 128], BF16))
        qbt = _st.enter_context(nc.sbuf_tensor("qbt", [128, 4, 128], BF16))
        pbt = _st.enter_context(nc.sbuf_tensor("pbt", [128, 4, 128], BF16))
        qibt = _st.enter_context(nc.sbuf_tensor("qibt", [128, 4, 128], BF16))
        wgba_sb = _st.enter_context(nc.sbuf_tensor("wgba_sb", [128, KC, 24], BF16))
        brr_sb = _st.enter_context(nc.sbuf_tensor("brr_sb", [128, 4], F32))
        mrep = _st.enter_context(nc.sbuf_tensor("mrep", [128, 512], BF16))
        wr32 = _st.enter_context(nc.sbuf_tensor("wr32", [128, KC, 4], F32))
        negcs = _st.enter_context(nc.sbuf_tensor("negcs", [1, 4], F32))
        rstdc = _st.enter_context(nc.sbuf_tensor("rstdc", [128, 1], F32))
        vones = _st.enter_context(nc.sbuf_tensor("vones", [8, 1], F32))
        c_ident = _st.enter_context(nc.sbuf_tensor("sb_c_ident", [128, 128], F32))
        c_identb = _st.enter_context(nc.sbuf_tensor("sb_c_identb", [128, 128], BF16))
        c_meanones = _st.enter_context(nc.sbuf_tensor("sb_c_meanones", [128, 1], F32))
        c_headsel = _st.enter_context(nc.sbuf_tensor("sb_c_headsel", [128, H], F32))
        c_selrow = _st.enter_context(nc.sbuf_tensor("sb_c_selrow", [H, 128], F32))
        be1c_sb = _st.enter_context(nc.sbuf_tensor("be1c_sb", [128, 32], F32))
        pq0 = _st.enter_context(nc.psum_tensor("pq0", [128, 512], F32))
        pq1 = _st.enter_context(nc.psum_tensor("pq1", [128, 512], F32))
        pq2 = _st.enter_context(nc.psum_tensor("pq2", [128, 512], F32))
        pq3 = _st.enter_context(nc.psum_tensor("pq3", [128, 512], F32))
        pq4 = _st.enter_context(nc.psum_tensor("pq4", [128, 512], F32))
        pq5 = _st.enter_context(nc.psum_tensor("pq5", [128, 512], F32))
        pq6 = _st.enter_context(nc.psum_tensor("pq6", [128, 512], F32))
        pq7 = _st.enter_context(nc.psum_tensor("pq7", [128, 512], F32))
        s_sync = _st.enter_context(nc.semaphore("s_sync"))
        s_gp = _st.enter_context(nc.semaphore("s_gpsimd"))
        s_t = _st.enter_context(nc.semaphore("s_tensor"))
        s_v = _st.enter_context(nc.semaphore("s_vector"))
        s_s = _st.enter_context(nc.semaphore("s_scalar"))
        sems = {"sync": s_sync, "gpsimd": s_gp, "tensor": s_t,
                "vector": s_v, "scalar": s_s}
        pk = [pq0, pq1, pq2, pq3, pq4, pq5, pq6, pq7]
        PKR = ["pq%d" % i for i in range(8)]
        sb_const = {"c_ident": c_ident, "c_meanones": c_meanones,
                    "c_selrow": c_selrow}

        def sdma(dst, src, reads, writes, tag=""):
            em.op("sync", lambda nc, eng, d=dst, s=src: eng.dma_start(d, s),
                  reads=reads, writes=writes, dma=True, tag=tag)

        def gdma(dst, src, reads, writes, tag=""):
            em.op("gpsimd", lambda nc, eng, d=dst, s=src: eng.dma_start(d, s),
                  reads=reads, writes=writes, dma=True, tag=tag)

        DBG = {}
        def ddump(name, src_ap, reads):
            if not dbg:
                return
            shp = list(src_ap.shape)
            dt_ = src_ap.dtype
            t_ = nc.dram_tensor("dbg_" + name, shp, dt_, kind="ExternalOutput").ap()
            DBG[name] = t_
            em.op("sync", lambda nc, eng, d=t_, s=src_ap: eng.dma_start(d, s),
                  reads=reads, writes=("dbg_" + name,), dma=True, tag="dbg_" + name)

        def mm(dst, lhsT, rhs, start, stop, reads, writes, tag=""):
            em.op("tensor",
                  lambda nc, eng, d=dst, l=lhsT, r=rhs, s1=start, s2=stop:
                  nc.tensor.matmul(d, l, r, start=s1, stop=s2),
                  reads=reads, writes=writes, tag=tag)

        def trans(dst, src, ident, reads, writes, tag=""):
            em.op("tensor",
                  lambda nc, eng, d=dst, s=src, i=ident:
                  nc.tensor.transpose(d, s, i),
                  reads=reads, writes=writes, tag=tag)

        def vop(fn, reads, writes, tag=""):
            em.op("vector", fn, reads=reads, writes=writes, tag=tag)

        def sop(fn, reads, writes, tag=""):
            em.op("scalar", fn, reads=reads, writes=writes, tag=tag)

        b0 = big0.ap()
        def xT_sl(kc, lo, hi):
            return b0[:, kc * T + lo: kc * T + hi]
        x2_sl = xT_sl
        RS = {"gate": 0, "beta": 1, "lnbeta": 2, "dec": 3, "sufexp": 4,
              "dexp": 5, "nrm": 6, "rwsh": 7}
        def row(name, lo=0, hi=None):
            hi = T if hi is None else hi
            return b0[0:H, RS[name] * T + lo: RS[name] * T + hi]

        b1 = big1.ap()
        def vT_sl(kc, lo, hi):
            return b1[:, kc * T + lo: kc * T + hi]
        def h1m_sl(fc, lo, hi):
            return b1[:, fc * TS + lo: fc * TS + hi]

        b2 = big2.ap()
        def kw_sl(h, lo, hi):
            return b2[:, h, lo:hi]
        def k_sl(h, lo, hi):
            return b2[:, h, 1 + lo:1 + hi]

        b3 = big3.ap()
        b3f = big3.ap().bitcast(F32)
        def oT_sl(kc, lo, hi):
            return b3[:, kc * T + lo: kc * T + hi]
        tmp_sq = b3f[:, 0:T]
        tmpA = b3f[:, TF:TF + 512]
        tmpF = b3f[:, 4 * TF:4 * TF + 512]
        def t1(lo=0, hi=None):
            hi = T if hi is None else hi
            return b3f[0:H, TF + lo: TF + hi]
        def t2(lo=0, hi=None):
            hi = T if hi is None else hi
            return b3f[0:H, 2 * TF + lo: 2 * TF + hi]
        def t3(lo=0, hi=None):
            hi = T if hi is None else hi
            return b3f[0:H, 3 * TF + lo: 3 * TF + hi]
        def srow(i, lo=0, hi=None, p=1):
            hi = T if hi is None else hi
            return b3f[0:p, (4 + i) * TF + lo: (4 + i) * TF + hi]
        L_OFF = 7 * TF
        def Lsl(fam, lo, hi):
            return b3f[:, L_OFF + (fam - 1) * 512 + lo: L_OFF + (fam - 1) * 512 + hi]
        def we1b(buf):
            return b3[:, buf * 4096:(buf + 1) * 4096].rearrange(
                "p (k f) -> p k f", k=KC)
        def we2b(buf):
            return b3[:, 8192 + buf * 4096: 8192 + (buf + 1) * 4096].rearrange(
                "p (f j) -> p f j", f=4)
        maskfm = b3[0:4, 28672:28672 + T]
        be2e_sb = b3[0:4, 31232:32256]
        sel4_sb = b3[0:4, 32256:32384]
        c_maskb = b3[:, 32384:32768]

        wst = wstr.ap()

        sdma(b0.rearrange("p (c t) -> p c t", c=8), dr["xT"].rearrange("(c p) t -> p c t", p=128),
             reads=(), writes=("xT",), tag="ld_x")
        for nm, sb in sb_const.items():
            if nm == "c_selrow":
                sdma(sb[:], dr[nm][:, 0:128], reads=(), writes=(nm,), tag="ld_selrow0")
                continue
            if nm == "c_maskb":
                gdma(sb[:], dr[nm][:], reads=(), writes=(nm,), tag="ld_" + nm)
            else:
                sdma(sb[:], dr[nm][:], reads=(), writes=(nm,), tag="ld_" + nm)
        sdma(brr_sb[:], dr["brr"][:], reads=(), writes=("brr",))
        sdma(dtb_sb[:], dr["dtb"][:], reads=(), writes=("dtb",))
        sdma(alg_sb[:], dr["alg"][:], reads=(), writes=("alg",))
        gdma(c_identb[:], dr["c_ident"][:], reads=(), writes=("c_identb",))
        gdma(c_maskb, dr["c_maskb"][:], reads=(), writes=("c_maskb",))
        gdma(wgba_sb[:], dr["wgbaT"].rearrange("(k p) c -> p k c", p=128),
             reads=(), writes=("wgba",))
        sdma(wr32[:], dr["wr"].rearrange("(k p) c -> p k c", p=128),
             reads=(), writes=("wr32",))
        vop(lambda nc, eng: nc.vector.memset(vones[:], 1.0),
            reads=(), writes=("vones",), tag="vones")
        sop(lambda nc, eng: nc.scalar.activation(expa[:], alg_sb[:], AF.Exp),
            reads=("alg",), writes=("expa",), tag="expa")
        sop(lambda nc, eng: nc.scalar.mul(ndtb[:], dtb_sb[:], -1.0),
            reads=("dtb",), writes=("ndtb",), tag="ndtb")

        def emit_ln(src_region, src_sl, dst_region, subkey=False, pad=True):
            for kc in range(KC):
                for t in range(NTS):
                    mm(pk[t][0:1, 0:TS], c_meanones[:, 0:1],
                       src_sl(kc, t * TS, (t + 1) * TS),
                       start=(kc == 0), stop=(kc == KC - 1),
                       reads=("c_meanones", src_region), writes=(PKR[t],),
                       tag="ln_mean")
            for kc in range(KC):
                sop(lambda nc, eng, s=src_sl(kc, 0, T):
                    nc.scalar.activation(tmp_sq[:, 0:T], s, AF.Square),
                    reads=(src_region,), writes=("sqtmp",), tag="ln_sq")
                for t in range(NTS):
                    mm(pk[4 + t][0:1, 0:TS], c_meanones[:, 0:1],
                       tmp_sq[:, t * TS:(t + 1) * TS],
                       start=(kc == 0), stop=(kc == KC - 1),
                       reads=("c_meanones", "sqtmp"), writes=(PKR[4 + t],),
                       tag="ln_ssq")
            for t in range(NTS):
                sop(lambda nc, eng, t_=t: nc.scalar.copy(
                    srow(0, t_ * TS, (t_ + 1) * TS), pk[t_][0:1, 0:TS]),
                    reads=(PKR[t],), writes=("stats",), tag="ln_mcp")
                sop(lambda nc, eng, t_=t: nc.scalar.copy(
                    srow(1, t_ * TS, (t_ + 1) * TS), pk[4 + t_][0:1, 0:TS]),
                    reads=(PKR[4 + t],), writes=("stats",), tag="ln_scp")
            vop(lambda nc, eng: nc.vector.tensor_mul(
                srow(2), srow(0), srow(0)),
                reads=("stats",), writes=("stats",), tag="ln_m2")
            vop(lambda nc, eng: nc.vector.tensor_sub(srow(1), srow(1), srow(2)),
                reads=("stats",), writes=("stats",), tag="ln_var")
            vop(lambda nc, eng: nc.vector.tensor_scalar_add(
                srow(1), srow(1), 1e-5),
                reads=("stats",), writes=("stats",), tag="ln_eps")
            sop(lambda nc, eng: nc.scalar.activation(
                srow(1), srow(1), AF.Sqrt),
                reads=("stats",), writes=("stats",), tag="ln_sqrt")
            vop(lambda nc, eng: nc.vector.reciprocal(srow(1), srow(1)),
                reads=("stats",), writes=("stats",), tag="ln_rcp")
            sdma(c_selrow[:], dr["c_selrow"][:, 0:128],
                 reads=(), writes=("c_selrow",), tag="ld_selrow_ln")
            for t in range(NTS):
                mm(pk[6][:, 0:TS], c_selrow[0:1, 0:128], srow(1, t * TS, (t + 1) * TS),
                   start=True, stop=True, reads=("c_selrow", "stats"),
                   writes=(PKR[6],), tag="ln_rrep")
                mm(pk[7][:, 0:TS], c_selrow[0:1, 0:128], srow(0, t * TS, (t + 1) * TS),
                   start=True, stop=True, reads=("c_selrow", "stats"),
                   writes=(PKR[7],), tag="ln_mrep")
                dkeys = (dst_region,) + ((("%st%d" % (dst_region, t),))
                                          if subkey else ())
                for kc in range(KC):
                    vop(lambda nc, eng, kc_=kc, t_=t,
                        s=src_sl(kc, t * TS, (t + 1) * TS):
                        nc.vector.tensor_sub(
                            b2[:, kc_, 1 + t_ * TS: 1 + (t_ + 1) * TS],
                            s, pk[7][:, 0:TS]),
                        reads=(src_region, PKR[7]), writes=dkeys, tag="ln_a1")
                    vop(lambda nc, eng, kc_=kc, t_=t:
                        nc.vector.tensor_mul(
                            b2[:, kc_, 1 + t_ * TS: 1 + (t_ + 1) * TS],
                            b2[:, kc_, 1 + t_ * TS: 1 + (t_ + 1) * TS],
                            pk[6][:, 0:TS]),
                        reads=dkeys + (PKR[6],), writes=dkeys, tag="ln_a2")
            if pad:
                for kc in range(KC):
                    vop(lambda nc, eng, kc_=kc: nc.vector.memset(b2[:, kc_, 0:1], 0.0),
                        reads=(), writes=(dst_region,), tag="ln_pad")

        emit_ln("xT", xT_sl, "ln1")
        def ln1_rhs(kc, lo, hi):
            return b2[:, kc, 1 + lo: 1 + hi]

        ddump("ln1", b2[:, :, :], ("ln1",))
        for jc in range(KC):
            buf = 0
            gdma(wst[:, buf], dr["wwT"].rearrange("(k p) j -> p k j", p=128)
                 [:, :, jc * 128:(jc + 1) * 128],
                 reads=(), writes=("ws%d" % buf,), tag="ld_ww")
            for t in range(NTS):
                pb, pbr = pk[(jc * NTS + t) % 2], PKR[(jc * NTS + t) % 2]
                for kc in range(KC):
                    mm(pb[:, 0:TS], wst[:, buf, kc, :],
                       ln1_rhs(kc, t * TS, (t + 1) * TS),
                       start=(kc == 0), stop=(kc == KC - 1),
                       reads=("ws%d" % buf, "ln1"), writes=(pbr,), tag="vproj")
                sop(lambda nc, eng, jc_=jc, t_=t, pb_=pb:
                    nc.scalar.copy(vT_sl(jc_, t_ * TS, (t_ + 1) * TS), pb_[:, 0:TS]),
                    reads=(pbr,), writes=("vT",), tag="vproj_ev")

        for t in range(NTS):
            for g in range(3):
                pb, pbr = pk[2 + g], PKR[2 + g]
                for kc in range(KC):
                    mm(pb[0:H, 0:TS], wgba_sb[:, kc, g * 8:(g + 1) * 8],
                       ln1_rhs(kc, t * TS, (t + 1) * TS),
                       start=(kc == 0), stop=(kc == KC - 1),
                       reads=("wgba", "ln1"), writes=(pbr,), tag="gba")
            t0w = ("xT",) if t == 0 else ()
            sop(lambda nc, eng, t_=t: nc.scalar.activation(
                row("gate", t_ * TS, (t_ + 1) * TS), pk[2][0:H, 0:TS], AF.Sigmoid),
                reads=(PKR[2],), writes=("rows",) + t0w, tag="gate")
            sop(lambda nc, eng, t_=t: nc.scalar.activation(
                row("beta", t_ * TS, (t_ + 1) * TS), pk[3][0:H, 0:TS], AF.Sigmoid),
                reads=(PKR[3],), writes=("rows",), tag="beta")
            sop(lambda nc, eng, t_=t: nc.scalar.activation(
                row("dec", t_ * TS, (t_ + 1) * TS), pk[4][0:H, 0:TS],
                AF.Sigmoid, bias=ndtb[:, 0:1], scale=-1.0),
                reads=(PKR[4], "ndtb"), writes=("rows",), tag="siga")

        sop(lambda nc, eng: nc.scalar.activation(row("lnbeta"), row("beta"), AF.Ln),
            reads=("rows",), writes=("rows",), tag="lnbeta")
        sop(lambda nc, eng: nc.scalar.activation(row("dec"), row("dec"), AF.Ln),
            reads=("rows",), writes=("rows",), tag="lnsig")
        vop(lambda nc, eng: nc.vector.tensor_scalar_mul(
            row("dec"), row("dec"), expa[:, 0:1]),
            reads=("rows", "expa"), writes=("rows",), tag="decay")
        ddump("decay", row("dec"), ("rows",))
        for n in range(NCH):
            vop(lambda nc, eng, n_=n: nc.vector.tensor_tensor_scan(
                row("dec", n_ * C, (n_ + 1) * C),
                vones[:, 0:1].broadcast_to([H, C]),
                row("dec", n_ * C, (n_ + 1) * C), 0.0, ALU.mult, ALU.add),
                reads=("rows", "vones"), writes=("rows",), tag="scan")
        ddump("cum", row("dec"), ("rows",))
        vop(lambda nc, eng: nc.vector.tensor_copy(bndt[:], row("dec")[:, C - 1::C]),
            reads=("rows",), writes=("bnd",), tag="bnd")
        vop(lambda nc, eng: nc.vector.tensor_tensor(
            row("sufexp").rearrange("p (n c) -> p n c", c=C),
            bndt[:, :, None].broadcast_to([H, NCH, C]),
            row("dec").rearrange("p (n c) -> p n c", c=C), ALU.subtract),
            reads=("rows", "bnd"), writes=("rows",), tag="sufdec")
        ddump("sufd", row("sufexp"), ("rows",))
        ddump("bndt", bndt[:, :], ("bnd",))
        sop(lambda nc, eng: nc.scalar.activation(row("sufexp"), row("sufexp"), AF.Exp),
            reads=("rows",), writes=("rows",), tag="sufexp")
        ddump("decf", row("dec"), ("rows",))
        sop(lambda nc, eng: nc.scalar.activation(row("dexp"), row("dec"), AF.Exp),
            reads=("rows",), writes=("rows",), tag="dexp")
        vop(lambda nc, eng: nc.vector.tensor_copy(sscale[:], row("dexp")[:, C - 1::C]),
            reads=("rows",), writes=("sscale",), tag="sscale")
        for h in range(H):
            sdma(c_selrow[:], dr["c_selrow"][:, h * 128:(h + 1) * 128],
                 reads=(), writes=("c_selrow",), tag="ld_selrow")
            mm(pk[5][:, 0:NCH], c_selrow[:, :], sscale[:],
               start=True, stop=True, reads=("c_selrow", "sscale"),
               writes=(PKR[5],), tag="sscmm")
            sop(lambda nc, eng, h_=h: nc.scalar.copy(
                ssc[:, h_ * NCH:(h_ + 1) * NCH], pk[5][:, 0:NCH]),
                reads=(PKR[5],), writes=("ssc",), tag="sscev")

        for kc in range(KC):
            sdma(c_headsel[:], dr["c_headsel"][:, kc * H:(kc + 1) * H],
                 reads=(), writes=("c_headsel",), tag="ld_hsel")
            sop(lambda nc, eng, kc_=kc: nc.scalar.activation(
                tmp_sq[:, 0:T], ln1_rhs(kc_, 0, T), AF.Square),
                reads=("ln1",), writes=("sqtmp",), tag="ksq")
            for t in range(NTS):
                mm(pk[4 + t][0:H, 0:TS], c_headsel[:, 0:H],
                   tmp_sq[:, t * TS:(t + 1) * TS],
                   start=(kc == 0), stop=(kc == KC - 1),
                   reads=("c_headsel", "sqtmp"), writes=(PKR[4 + t],), tag="ssqmm")
        for t in range(NTS):
            sop(lambda nc, eng, t_=t: nc.scalar.activation(
                row("nrm", t_ * TS, (t_ + 1) * TS), pk[4 + t_][0:H, 0:TS], AF.Sqrt),
                reads=(PKR[4 + t],), writes=("rows",), tag="nrm")
        vop(lambda nc, eng: nc.vector.tensor_scalar_max(row("nrm"), row("nrm"), 1e-12),
            reads=("rows",), writes=("rows",), tag="nrmmax")
        vop(lambda nc, eng: nc.vector.reciprocal(row("nrm"), row("nrm")),
            reads=("rows",), writes=("rows",), tag="rnorm")
        sop(lambda nc, eng: nc.scalar.activation(t1(), row("nrm"), AF.Ln),
            reads=("rows",), writes=("rowsB",), tag="lnr")
        vop(lambda nc, eng: nc.vector.memset(row("rwsh", 0, 1), 0.0),
            reads=(), writes=("rows",))
        vop(lambda nc, eng: nc.vector.tensor_copy(
            row("rwsh", 1, T), row("nrm", 0, T - 1)),
            reads=("rows",), writes=("rows",), tag="rwsh")
        vop(lambda nc, eng: nc.vector.memset(t2(0, 1), NEG),
            reads=(), writes=("rowsB",))
        vop(lambda nc, eng: nc.vector.tensor_copy(t2(1, T), t1(0, T - 1)),
            reads=("rowsB",), writes=("rowsB",), tag="lnrwsh")
        vop(lambda nc, eng: nc.vector.tensor_sub(
            srow(0, p=H), row("dec"), t2()),
            reads=("rows", "rowsB"), writes=("stats",), tag="pld1")
        vop(lambda nc, eng: nc.vector.tensor_add(
            row("lnbeta"), row("dec"), row("lnbeta")),
            reads=("rows",), writes=("rows",), tag="pld2a")
        vop(lambda nc, eng: nc.vector.tensor_add(
            srow(1, p=H), row("lnbeta"), t2()),
            reads=("rows", "rowsB"), writes=("stats",), tag="pld2")
        vop(lambda nc, eng: nc.vector.tensor_add(
            srow(2, p=H), row("dec"), t1()),
            reads=("rows", "rowsB"), writes=("stats",), tag="pld3")
        vop(lambda nc, eng: nc.vector.tensor_mul(t3(), row("gate"), row("dexp")),
            reads=("rows",), writes=("rowsB",), tag="ogr_a")
        vop(lambda nc, eng: nc.vector.tensor_mul(t3(), t3(), row("nrm")),
            reads=("rows", "rowsB"), writes=("rowsB",), tag="ogr")
        vop(lambda nc, eng: nc.vector.scalar_tensor_tensor(
            row("lnbeta"), row("beta"), -1.0, row("rwsh"), ALU.mult, ALU.mult),
            reads=("rows",), writes=("rows",), tag="negbrw")
        vop(lambda nc, eng: nc.vector.tensor_mul(
            row("dexp"), row("lnbeta"), row("dexp")),
            reads=("rows",), writes=("rows",), tag="negbdrw")
        vop(lambda nc, eng: nc.vector.tensor_mul(
            row("sufexp"), row("sufexp"), row("rwsh")),
            reads=("rows",), writes=("rows",), tag="bw")

        ddump("rows", b0[0:8, :], ("rows",))
        ddump("pld1", srow(0, 0, T, p=H), ("stats",))
        ddump("pld2", srow(1, 0, T, p=H), ("stats",))
        ddump("pld3", srow(2, 0, T, p=H), ("stats",))
        ddump("vT", b1[:, 0:8 * T], ("vT",))
        ddump("sscv", ssc[:, :], ("ssc",))
        TSRC = [lambda n: row("dec", n * C, (n + 1) * C),
                lambda n: srow(0, n * C, (n + 1) * C, p=H),
                lambda n: row("lnbeta", n * C, (n + 1) * C),
                lambda n: row("dexp", n * C, (n + 1) * C),
                lambda n: row("gate", n * C, (n + 1) * C),
                lambda n: row("beta", n * C, (n + 1) * C),
                lambda n: t3(n * C, (n + 1) * C),
                lambda n: row("sufexp", n * C, (n + 1) * C)]
        TSRC_R = ["rows", "stats", "rows", "rows", "rows", "rows", "rowsB", "rows"]
        def tscl_col(h, n, s):
            if s < 5:
                i = (n * 5 + s) * 8 + h
                return tscl32[:, i:i + 1]
            i = (n * 3 + (s - 5)) * 8 + h
            return tscl16[:, i:i + 1]
        def tscl_colb(h, n, s):
            return tscl_col(h, n, s)[:, 0:1].broadcast_to([128, 128])
        ltm_ = tscl32[:, 0:NTT * 4]
        mtm_ = tscl32[:, NTT * 4:NTT * 8 + 1]
        for n in range(NCH):
            for s in range(8):
                trans(pk[3][:, s * 8:(s + 1) * 8], TSRC[s](n), c_ident[0:H, 0:H],
                      reads=(TSRC_R[s], "c_ident"), writes=(PKR[3],), tag="tsclT")
            sop(lambda nc, eng, n_=n: nc.scalar.copy(
                tscl32[:, n_ * 40:(n_ + 1) * 40], pk[3][:, 0:40]),
                reads=(PKR[3],), writes=("tscl",), tag="tscl_ev32")
            sop(lambda nc, eng, n_=n: nc.scalar.copy(
                tscl16[:, n_ * 24:(n_ + 1) * 24], pk[3][:, 40:64]),
                reads=(PKR[3],), writes=("tscl",), tag="tscl_ev16")

        ddump("tscl32", tscl32[:, :], ("tscl",))
        ddump("tscl16", tscl16[:, :], ("tscl",))
        for h in range(H):
            sdma(c_selrow[:], dr["c_selrow"][:, h * 128:(h + 1) * 128],
                 reads=(), writes=("c_selrow",), tag="ld_selrowh")
            vop(lambda nc, eng: nc.vector.memset(Ssta[:], 0.0),
                reads=(), writes=("S",), tag="S0")
            vop(lambda nc, eng: nc.vector.memset(Sb[:], 0.0),
                reads=(), writes=("Sbfr",), tag="Sb0")
            for n in range(NCH):
                tsl, ns = n // 4, n % 4
                par = (h * NCH + n) % 2
                pG, pGr = (pk[1], PKR[1]) if par == 0 else (pk[5], PKR[5])
                pT, pTr = (pk[3], PKR[3]) if par == 0 else (pk[6], PKR[6])
                QS = 2 * par
                if ns == 0:
                    lo = tsl * 512
                    wid = min(512, T - lo)
                    nch_in = wid // C
                    for fam, pldrow, moff, colslot in (
                            (1, 0, 0, 0), (2, 1, 128, 1), (3, 2, 256, 1)):
                        for q in range(wid // C):
                            mm(pk[0][:, q * C:(q + 1) * C], c_identb[:, :],
                               c_maskb[:, moff:moff + C],
                               start=(q == 0), stop=False,
                               reads=("c_identb", "c_maskb"), writes=(PKR[0],),
                               tag="Lmask")
                        mm(pk[0][:, 0:wid], c_selrow[:, :],
                           srow(pldrow, lo, lo + wid, p=H),
                           start=False, stop=True,
                           reads=("c_selrow", "stats"), writes=(PKR[0],),
                           tag="Lpld")
                        i0 = ((tsl * 4) * 5 + colslot) * 8 + h
                        col = tscl32[:, i0: i0 + (nch_in - 1) * 40 + 1: 40]
                        colb = col[:, :, None].broadcast_to([128, nch_in, C])
                        if fam == 1:
                            vop(lambda nc, eng, d=colb, f=fam, w=wid:
                                nc.vector.tensor_tensor(
                                    Lsl(f, 0, w).rearrange("p (a b) -> p a b", b=C),
                                    d,
                                    pk[0][:, 0:w].rearrange("p (a b) -> p a b", b=C),
                                    ALU.subtract),
                                reads=("tscl", PKR[0]), writes=("Lz",), tag="Lsub")
                        else:
                            vop(lambda nc, eng, d=colb, f=fam, w=wid:
                                nc.vector.tensor_tensor(
                                    Lsl(f, 0, w).rearrange("p (a b) -> p a b", b=C),
                                    pk[0][:, 0:w].rearrange("p (a b) -> p a b", b=C),
                                    d, ALU.subtract),
                                reads=("tscl", PKR[0]), writes=("Lz",), tag="Lsub")
                        sop(lambda nc, eng, f=fam, w=wid: nc.scalar.activation(
                            Lsl(f, 0, w), Lsl(f, 0, w), AF.Exp),
                            reads=("Lz",), writes=("Lz",), tag="Lexp")
                a = n * C
                mm(pG[:, 0:128], kw_sl(h, a, a + C), kw_sl(h, a, a + C),
                   True, True, reads=("ln1",), writes=(pGr,), tag="G")
                mm(pG[:, 128:256], kw_sl(h, a, a + C), k_sl(h, a, a + C),
                   True, True, reads=("ln1",), writes=(pGr,), tag="attnT")
                trans(pT[:, 0:64].bitcast(BF16), vT_sl(h, a, a + C), c_identb[:],
                      reads=("vT", "c_identb"), writes=(pTr,), tag="trV")
                trans(pT[:, 64:128].bitcast(BF16), kw_sl(h, a, a + C), c_identb[:],
                      reads=("ln1", "c_identb"), writes=(pTr,), tag="trKw")
                mm(pk[2][:, 0:128], kw_sl(h, a, a + C), Sb[:],
                   True, True, reads=("ln1", "Sbfr"), writes=(PKR[2],), tag="KwS")
                mm(pk[2][:, 128:256], k_sl(h, a, a + C), Sb[:],
                   True, True, reads=("ln1", "Sbfr"), writes=(PKR[2],), tag="KS")
                Lq = lambda f, ns_=ns: Lsl(f, ns_ * C, ns_ * C + C)
                ot_extra = ("sqtmp", "rowsB") if (h == 0 and n == 0) else ()
                vop(lambda nc, eng, c1=tscl_col(h, n, 2), L1q=Lq(1), s=QS, pG_=pG:
                    nc.vector.scalar_tensor_tensor(
                        pbt[:, s], pG_[:, 0:128], c1, L1q, ALU.mult, ALU.mult),
                    reads=(pGr, "tscl", "Lz"), writes=("P%d" % QS,), tag="negM")
                vop(lambda nc, eng, L2q=Lq(2), s=QS, pG_=pG:
                    nc.vector.scalar_tensor_tensor(
                        qbt[:, s], pG_[:, 0:128], -1.0, L2q, ALU.mult, ALU.mult),
                    reads=(pGr, "Lz"), writes=("Q%d" % QS,), tag="negMT")
                vop(lambda nc, eng, s=QS: nc.vector.tensor_add(
                    qibt[:, s], qbt[:, s], c_identb[:]),
                    reads=("Q%d" % QS, "c_identb"), writes=("QI%d" % QS,), tag="QI0")
                vop(lambda nc, eng, L3q=Lq(3), pG_=pG: nc.vector.tensor_mul(
                    attm[:], pG_[:, 128:256], L3q),
                    reads=(pGr, "Lz"), writes=("attm",), tag="attnTm")
                vop(lambda nc, eng, c1=tscl_colb(h, n, 5), pT_=pT:
                    nc.vector.tensor_tensor(u1t[:], pT_[:, 0:64].bitcast(BF16),
                                            c1, ALU.mult),
                    reads=(pTr, "tscl"), writes=("u1t",), tag="u1")
                vop(lambda nc, eng, c1=tscl_col(h, n, 3):
                    nc.vector.scalar_tensor_tensor(
                        vvt[:, 0], pk[2][:, 0:128], c1, u1t[:], ALU.mult, ALU.add),
                    reads=(PKR[2], "tscl", "u1t"), writes=("vv0",), tag="U")
                vop(lambda nc, eng, c1=tscl_colb(h, n, 7), pT_=pT:
                    nc.vector.tensor_tensor(kwdw[:], pT_[:, 64:128].bitcast(BF16),
                                            c1, ALU.mult),
                    reads=(pTr, "tscl"), writes=("kwdw",), tag="kwdw")
                for m in range(NINV):
                    cur, nxt = m % 2, 1 - m % 2
                    mm(pk[4][:, 0:128], qibt[:, QS + cur], vvt[:, cur],
                       True, True, reads=("QI%d" % (QS + cur), "vv%d" % cur),
                       writes=(PKR[4],), tag="apply")
                    vop(lambda nc, eng, nx=nxt: nc.vector.tensor_copy(
                        vvt[:, nx], pk[4][:, 0:128]),
                        reads=(PKR[4],), writes=("vv%d" % nxt,), tag="vv_ev")
                    if m < NINV - 1:
                        mm(pT[:, 128:256], pbt[:, QS + cur], qbt[:, QS + cur],
                           True, True,
                           reads=("P%d" % (QS + cur), "Q%d" % (QS + cur)),
                           writes=(pTr,), tag="Qsq")
                        vop(lambda nc, eng, nx=nxt, pT_=pT, s=QS:
                            nc.vector.tensor_add(
                                qibt[:, s + nx], pT_[:, 128:256], c_identb[:]),
                            reads=(pTr, "c_identb"), writes=("QI%d" % (QS + nxt),),
                            tag="QIev")
                        if m < NINV - 2:
                            mm(pT[:, 256:384], qbt[:, QS + cur], pbt[:, QS + cur],
                               True, True,
                               reads=("P%d" % (QS + cur), "Q%d" % (QS + cur)),
                               writes=(pTr,), tag="Psq")
                            sop(lambda nc, eng, nx=nxt, pT_=pT, s=QS:
                                nc.scalar.copy(qbt[:, s + nx], pT_[:, 128:256]),
                                reads=(pTr,), writes=("Q%d" % (QS + nxt),), tag="Qev")
                            sop(lambda nc, eng, nx=nxt, pT_=pT, s=QS:
                                nc.scalar.copy(pbt[:, s + nx], pT_[:, 256:384]),
                                reads=(pTr,), writes=("P%d" % (QS + nxt),), tag="Pev")
                vfin = "vv%d" % (NINV % 2)
                vfin_ap = vvt[:, NINV % 2]
                mm(pk[7][:, 0:128], attm[:], vfin_ap, True, True,
                   reads=("attm", vfin), writes=(PKR[7],), tag="o2")
                vop(lambda nc, eng, c1=tscl_colb(h, n, 6): nc.vector.tensor_tensor(
                    o1t[:], pk[2][:, 128:256], c1, ALU.mult),
                    reads=(PKR[2], "tscl"), writes=("o1t",), tag="o1")
                vop(lambda nc, eng, c1=tscl_col(h, n, 4):
                    nc.vector.scalar_tensor_tensor(
                        otm[:], pk[7][:, 0:128], c1, o1t[:], ALU.mult, ALU.add),
                    reads=(PKR[7], "tscl", "o1t"), writes=("otm",), tag="otm")
                trans(pk[7][:, 128:192].bitcast(BF16), otm[:], c_identb[:],
                      reads=("otm", "c_identb"), writes=(PKR[7],), tag="trO")
                sop(lambda nc, eng, h_=h, a_=a: nc.scalar.copy(
                    oT_sl(h_, a_, a_ + C), pk[7][:, 128:192].bitcast(BF16)),
                    reads=(PKR[7],), writes=("oT",) + ot_extra, tag="oT_ev")
                mm(pk[7][:, 256:384], kwdw[:], vfin_ap, True, True,
                   reads=("kwdw", vfin), writes=(PKR[7],), tag="B")
                vop(lambda nc, eng, h_=h, n_=n:
                    nc.vector.scalar_tensor_tensor(
                        Ssta[:], Ssta[:], ssc[:, h_ * NCH + n_: h_ * NCH + n_ + 1],
                        pk[7][:, 256:384], ALU.mult, ALU.add),
                    reads=("S", "ssc", PKR[7]), writes=("S",), tag="Supd")
                sop(lambda nc, eng: nc.scalar.copy(Sb[:], Ssta[:]),
                    reads=("S",), writes=("Sbfr",), tag="Sbf")

        sdma(b0.rearrange("p (c t) -> p c t", c=8), dr["xT"].rearrange("(c p) t -> p c t", p=128),
             reads=(), writes=("x2", "rows", "xT"), tag="ld_x2")
        for jc in range(KC):
            buf = 0
            gdma(wst[:, buf], dr["woT"].rearrange("(k p) j -> p k j", p=128)
                 [:, :, jc * 128:(jc + 1) * 128],
                 reads=(), writes=("ws%d" % buf,), tag="ld_wo")
            for t in range(NTS):
                pb, pbr = pk[(jc * NTS + t) % 2], PKR[(jc * NTS + t) % 2]
                for kc in range(KC):
                    mm(pb[:, 0:TS], wst[:, buf, kc, :],
                       oT_sl(kc, t * TS, (t + 1) * TS),
                       start=(kc == 0), stop=(kc == KC - 1),
                       reads=("ws%d" % buf, "oT"), writes=(pbr,), tag="oproj")
                vop(lambda nc, eng, jc_=jc, t_=t, pb_=pb:
                    nc.vector.tensor_add(
                        x2_sl(jc_, t_ * TS, (t_ + 1) * TS),
                        pb_[:, 0:TS], x2_sl(jc_, t_ * TS, (t_ + 1) * TS)),
                    reads=(pbr, "x2"), writes=("x2",), tag="res1")
                vop(lambda nc, eng, jc_=jc, t_=t:
                    nc.vector.tensor_add(
                        x2_sl(jc_, t_ * TS, (t_ + 1) * TS),
                        ln1_rhs(jc_, t_ * TS, (t_ + 1) * TS),
                        x2_sl(jc_, t_ * TS, (t_ + 1) * TS)),
                    reads=("ln1", "x2"), writes=("x2",), tag="res1b")

        ddump("x2a", b0[:, 0:8 * T], ("x2",))
        emit_ln("x2", x2_sl, "ln1", subkey=True, pad=False)
        def ln2_rhs(kc, lo, hi):
            return b2[:, kc, 1 + lo: 1 + hi]

        for kc in range(KC):
            mm(pk[6][0:1, 0:4], c_meanones[:, 0:1], wr32[:, kc, :],
               start=(kc == 0), stop=(kc == KC - 1),
               reads=("c_meanones", "wr32"), writes=(PKR[6],), tag="colsum")
        vop(lambda nc, eng: nc.vector.tensor_scalar_mul(
            negcs[:], pk[6][0:1, 0:4], -float(D)),
            reads=(PKR[6],), writes=("negcs",), tag="negcs")
        for tt in range(NTT):
            pb, pbr = pk[2 + tt % 2], PKR[2 + tt % 2]
            for kc in range(KC):
                mm(pb[0:128, 0:4],
                   x2_sl(kc, tt * 128, (tt + 1) * 128),
                   wr32[:, kc, :], start=(kc == 0), stop=False,
                   reads=("x2", "wr32"), writes=(pbr,), tag="router")
            mm(pb[0:128, 0:4], srow(0, tt * 128, (tt + 1) * 128),
               negcs[:], start=False, stop=True,
               reads=("stats", "negcs"), writes=(pbr,), tag="router_m")
            trans(pk[4 + tt % 2][:, 0:1], srow(1, tt * 128, (tt + 1) * 128),
                  c_ident[0:1, 0:1],
                  reads=("stats", "c_ident"), writes=(PKR[4 + tt % 2],),
                  tag="trRstd")
            sop(lambda nc, eng, tt_=tt: nc.scalar.copy(
                rstdc[:], pk[4 + tt_ % 2][:, 0:1]),
                reads=(PKR[4 + tt % 2],), writes=("rstdc",), tag="rstdev")
            vop(lambda nc, eng, tt_=tt, pb_=pb: nc.vector.scalar_tensor_tensor(
                ltm_[:, tt_ * 4:(tt_ + 1) * 4], pb_[0:128, 0:4], rstdc[:, 0:1],
                brr_sb[:], ALU.mult, ALU.add),
                reads=(pbr, "rstdc", "brr"),
                writes=("ltm",) + (("tscl",) if tt == 0 else ()), tag="logits")
        for tt in range(NTT):
            base = tt * 4
            for e in range(4):
                first = True
                for j in range(4):
                    if j == e:
                        continue
                    opc = ALU.is_gt if j < e else ALU.is_ge
                    if first:
                        vop(lambda nc, eng, b_=base, e_=e, j_=j, op_=opc:
                            nc.vector.tensor_tensor(
                                mtm_[:, b_ + e_:b_ + e_ + 1],
                                ltm_[:, b_ + e_:b_ + e_ + 1],
                                ltm_[:, b_ + j_:b_ + j_ + 1], op_),
                            reads=("ltm",), writes=("mtm",), tag="cmp")
                        first = False
                    else:
                        vop(lambda nc, eng, b_=base, e_=e, j_=j, op_=opc:
                            nc.vector.tensor_tensor(
                                mtm_[:, NTT * 4:NTT * 4 + 1],
                                ltm_[:, b_ + e_:b_ + e_ + 1],
                                ltm_[:, b_ + j_:b_ + j_ + 1], op_),
                            reads=("ltm",), writes=("mtmt",), tag="cmp")
                        vop(lambda nc, eng, b_=base, e_=e:
                            nc.vector.tensor_mul(
                                mtm_[:, b_ + e_:b_ + e_ + 1],
                                mtm_[:, b_ + e_:b_ + e_ + 1],
                                mtm_[:, NTT * 4:NTT * 4 + 1]),
                            reads=("mtm", "mtmt"), writes=("mtm",), tag="cmpm")
            trans(pk[4][0:4, 0:128], mtm_[:, base:base + 4], c_ident[:],
                  reads=("mtm", "c_ident"), writes=(PKR[4],), tag="trM")
            sop(lambda nc, eng, tt_=tt: nc.scalar.copy(
                maskfm[:, tt_ * 128:(tt_ + 1) * 128], pk[4][0:4, 0:128]),
                reads=(PKR[4],),
                writes=("maskfm", "mfmt%d" % (tt // 4)) +
                       (("Lz",) if tt == 0 else ()), tag="maskev")

        ddump("ltm", ltm_[:, :], ("ltm",))
        ddump("maskfm", maskfm[:, :], ("maskfm",))
        for e in range(4):
            sdma(be1c_sb[:], dr["be1c"][:, e * 32:(e + 1) * 32],
                 reads=(), writes=("be1c",), tag="ld_be1c")
            gdma(be2e_sb[:], dr["be2e"].rearrange("e (f j) -> e f j", f=4)[:, e],
                 reads=(), writes=("be2e",) + (("Lz",) if e == 0 else ()),
                 tag="ld_be2e")
            gdma(sel4_sb[:], dr["c_sel4"].rearrange("e (f j) -> e f j", f=4)[:, e],
                 reads=(), writes=("sel4",), tag="ld_sel4")
            for t in range(NTS):
                for fcb in range(8):
                    buf = fcb % 2
                    first_ld = (e == 0 and t == 0 and fcb < 2)
                    gdma(we1b(buf),
                         dr["we1"][e].rearrange("(k p) f -> p k f", p=128)
                         [:, :, fcb * 512:(fcb + 1) * 512],
                         reads=(),
                         writes=("we1b%d" % buf,) + (("oT", "sqtmp", "rowsB") if first_ld else ()),
                         tag="ld_we1")
                    for fcl in range(4):
                        fc = fcb * 4 + fcl
                        pb, pbr = pk[fc % 2], PKR[fc % 2]
                        for kc in range(KC):
                            mm(pb[:, 0:TS], we1b(buf)[:, kc, fcl * 128:(fcl + 1) * 128],
                               ln2_rhs(kc, t * TS, (t + 1) * TS),
                               start=(kc == 0), stop=(kc == KC - 1),
                               reads=("we1b%d" % buf, "ln1t%d" % t),
                               writes=(pbr,), tag="h1mm")
                        h1_extra = ("vT",) if (e == 0 and t == 0 and fc == 0) else ()
                        sop(lambda nc, eng, fc_=fc, e_=e, pb_=pb:
                            nc.scalar.activation(
                                h1m_sl(fc_, 0, TS), pb_[:, 0:TS], AF.Relu,
                                bias=be1c_sb[:, fc_:fc_ + 1]),
                            reads=(pbr, "be1c"), writes=("h1m",) + h1_extra,
                            tag="relu")
                if dbg and t == 0 and e < 2:
                    ddump("h1m_e%d" % e, b1[:, 0:32 * TS], ("h1m",))
                mm(pk[0][:, 0:TS], sel4_sb,
                   maskfm[:, t * TS:(t + 1) * TS], start=True, stop=True,
                   reads=("sel4", "mfmt%d" % t), writes=(PKR[0],), tag="mrepmm")
                sop(lambda nc, eng: nc.scalar.copy(mrep[:, 0:TS], pk[0][:, 0:TS]),
                    reads=(PKR[0],), writes=("mrep",), tag="mrepev")
                for fcb in range(8):
                    buf = fcb % 2
                    first_ld = (e == 0 and t == 0 and fcb < 2)
                    gdma(we2b(buf),
                         dr["we2"][e].rearrange("(f p) j -> p f j", p=128)
                         [:, fcb * 4:(fcb + 1) * 4, :],
                         reads=(),
                         writes=("we2b%d" % buf,) + (("oT", "rowsB") if first_ld else ()),
                         tag="ld_we2")
                    for jc in range(KC):
                        if fcb == 0:
                            mm(pk[jc][:, 0:TS], be2e_sb[:, jc * 128:(jc + 1) * 128],
                               maskfm[:, t * TS:(t + 1) * TS],
                               start=True, stop=False,
                               reads=("be2e", "mfmt%d" % t), writes=(PKR[jc],),
                               tag="oebias")
                        for fcl in range(4):
                            fc = fcb * 4 + fcl
                            mm(pk[jc][:, 0:TS],
                               we2b(buf)[:, fcl, jc * 128:(jc + 1) * 128],
                               h1m_sl(fc, 0, TS),
                               start=False, stop=(fcb == 7 and fcl == 3),
                               reads=("we2b%d" % buf, "h1m"), writes=(PKR[jc],),
                               tag="oemm")
                for jc in range(KC):
                    vop(lambda nc, eng, jc_=jc: nc.vector.tensor_mul(
                        tmpF[:, 0:TS], pk[jc_][:, 0:TS], mrep[:, 0:TS]),
                        reads=(PKR[jc], "mrep"), writes=("stats",), tag="oemask")
                    vop(lambda nc, eng, jc_=jc, t_=t: nc.vector.tensor_add(
                        x2_sl(jc_, t_ * TS, (t_ + 1) * TS),
                        tmpF[:, 0:TS], x2_sl(jc_, t_ * TS, (t_ + 1) * TS)),
                        reads=("stats", "x2"), writes=("x2",), tag="oeev")
                if dbg and t == 0 and e == 0:
                    ddump("x2e0", b0[:, 0:8 * T], ("x2",))

        sdma(out_d.rearrange("(c p) t -> p c t", p=128), b0.rearrange("p (c t) -> p c t", c=8),
             reads=("x2",), writes=("outd",), tag="st_out")

        dma_keys = em.finalize()
        for k in dma_keys:
            sems[k] = _st.enter_context(nc.semaphore(k))
        block = _st.enter_context(nc.Block())
        block.sync(lambda eng: em.run_engine(nc, eng, "sync", sems))
        block.gpsimd(lambda eng: em.run_engine(nc, eng, "gpsimd", sems))
        block.tensor(lambda eng: em.run_engine(nc, eng, "tensor", sems))
        block.vector(lambda eng: em.run_engine(nc, eng, "vector", sems))
        block.scalar(lambda eng: em.run_engine(nc, eng, "scalar", sems))

    return nc


def make_in_maps(inputs, T, ncores):
    shared = host_shared(inputs)
    consts = host_constants()
    x = np.asarray(inputs["L0_attn_x"])
    in_maps = []
    for b in range(ncores):
        m = {}
        for k, v in consts.items():
            m[k] = np.ascontiguousarray(v, dtype=np.float32)
        for k, v in shared.items():
            m[k] = np.ascontiguousarray(v, dtype=np.float32)
        m["xT"] = np.ascontiguousarray(x[b, :T, :].T, dtype=np.float32)
        in_maps.append(m)
    return in_maps


def kernel(**inputs):
    T = 2048
    nc = build_nc(T)
    in_maps = make_in_maps(inputs, T, 8)
    res = run_bass_kernel_spmd(nc, in_maps, core_ids=list(range(8)))
    out = np.stack([np.asarray(r["out"]).T for r in res.results], axis=0)
    return np.ascontiguousarray(out, dtype=np.float32)
